# version 04
# baseline (speedup 1.0000x reference)
import numpy as np
import ml_dtypes
import contextlib
import concourse.bass as bass
import concourse.mybir as mybir
from concourse.bass_utils import run_bass_kernel_spmd

F32 = mybir.dt.float32
BF16 = mybir.dt.bfloat16
AF = mybir.ActivationFunctionType
ALU = mybir.AluOpType
AXX = mybir.AxisListType.X

NCORES = 8
D = 1024
TOK = 2048
SEQ = 8192
DC = D // 128
NT = TOK // 512
GROUPS_BATCH = [[0, 1, 2, 3], [4, 5, 6, 7]]
GROUP_ALL = [[0, 1, 2, 3, 4, 5, 6, 7]]

ENGS = ("t", "v", "s", "p", "q")

ARENA_B_COLS = 50176
ARENA_F_COLS = 24832


def np_bf16(x):
    return np.ascontiguousarray(x).astype(ml_dtypes.bfloat16)


class _Handle:
    __slots__ = ("ctx", "rec")

    def __init__(self, ctx, rec):
        self.ctx = ctx
        self.rec = rec

    def inc(self, sem, n=1):
        self.rec[2].append((sem, n))
        self.ctx.semctr[sem] = self.ctx.semctr.get(sem, 0) + n
        return (sem, self.ctx.semctr[sem])


class _EngProxy:
    def __init__(self, ctx, eng):
        self._ctx = ctx
        self._eng = eng

    def __getattr__(self, fn):
        ctx, engname = self._ctx, self._eng

        def record(*args, **kwargs):
            rec = ["op", (fn, args, kwargs), []]
            ctx.streams[engname].append(rec)
            return _Handle(ctx, rec)

        return record

    def wait(self, sem, cnt):
        self._ctx.streams[self._eng].append(["wait", (sem, cnt), []])

    def wait_tok(self, tok):
        if tok is not None:
            self.wait(tok[0], tok[1])


class _Pool:

    def __init__(self, tensor, cols):
        self.t = tensor
        self.free = [(0, cols)]
        self.allocs = {}

    def alloc(self, name, cols, align=8):
        for i, (off, length) in enumerate(self.free):
            aoff = (off + align - 1) // align * align
            pad = aoff - off
            if length >= cols + pad:
                if pad:
                    self.free[i] = (off, pad)
                    self.free.insert(i + 1, (aoff + cols, length - pad - cols))
                    if self.free[i + 1][1] == 0:
                        self.free.pop(i + 1)
                else:
                    rem = (off + cols, length - cols)
                    if rem[1]:
                        self.free[i] = rem
                    else:
                        self.free.pop(i)
                self.allocs[name] = (aoff, cols)
                return self.t[:, aoff:aoff + cols]
        raise MemoryError(f"arena full allocating {name} ({cols} cols); "
                          f"free={self.free}")

    def release(self, name):
        off, cols = self.allocs.pop(name)
        self.free.append((off, cols))
        self.free.sort()
        merged = []
        for seg in self.free:
            if merged and merged[-1][0] + merged[-1][1] == seg[0]:
                merged[-1] = (merged[-1][0], merged[-1][1] + seg[1])
            else:
                merged.append(list(seg))
        self.free = [tuple(s) for s in merged]


class Ctx:
    def __init__(self, nc, stack, ab_cols=None, af_cols=None):
        self.nc = nc
        self.stack = stack
        self.streams = {e: [] for e in ENGS}
        self.semctr = {}
        self.sems = {}
        self.phase_idx = 0
        self._ncoll = 0
        self.phase_sems = {e: self.sem(f"ph_{e}") for e in ENGS}
        self.dma_sems = {e: self.sem(f"dma_{e}") for e in ("q", "p", "s")}
        for e in ENGS:
            setattr(self, e, _EngProxy(self, e))
        self.psum = [
            stack.enter_context(nc.psum_tensor(f"pb{i}", [128, 512], F32))
            for i in range(8)
        ]
        self.bank_tok = [None] * 8
        self.dram_t = {}
        ab_cols = ab_cols or ARENA_B_COLS
        af_cols = af_cols or ARENA_F_COLS
        arena_b = stack.enter_context(nc.sbuf_tensor("arena_b", [128, ab_cols], BF16))
        arena_f = stack.enter_context(nc.sbuf_tensor("arena_f", [128, af_cols], F32))
        self.pb = _Pool(arena_b, ab_cols)
        self.pf = _Pool(arena_f, af_cols)
        self.ident_f = stack.enter_context(nc.sbuf_tensor("ident_f", [128, 128], F32))
        self.ones_bc = stack.enter_context(nc.sbuf_tensor("ones_bc", [1, 128], F32))
        self.onesb_col = stack.enter_context(nc.sbuf_tensor("onesb_col", [128, 1], BF16))
        self.bar_dummy = stack.enter_context(nc.sbuf_tensor("bar_dummy", [1, 8], F32))
        self.ccol = stack.enter_context(nc.sbuf_tensor("ccol", [128, 16], F32))
        self.cc_map = {}

    def cc(self, value, parts=128):
        if value not in self.cc_map:
            i = len(self.cc_map)
            assert i < 16
            self.cc_map[value] = i
            self.v.memset(self.ccol[:, i:i + 1], value)
        i = self.cc_map[value]
        return self.ccol[0:parts, i:i + 1]

    def ab(self, name, cols):
        return self.pb.alloc(name, cols)

    def af(self, name, cols):
        return self.pf.alloc(name, cols)

    def fb(self, *names):
        for n in names:
            self.pb.release(n)

    def ff(self, *names):
        for n in names:
            self.pf.release(n)

    def sem(self, name):
        if name not in self.sems:
            self.sems[name] = self.stack.enter_context(self.nc.semaphore(name))
            self.semctr[self.sems[name]] = 0
        return self.sems[name]

    def dram(self, name, shape, dtype, kind="Internal"):
        if name not in self.dram_t:
            self.dram_t[name] = self.nc.dram_tensor(
                name, list(shape), dtype, kind=kind
            ).ap()
        return self.dram_t[name]

    def dma(self, queue, out, in_, **kw):
        sem = self.dma_sems[queue]
        h = getattr(self, queue).dma_start(out=out, in_=in_, **kw)
        return h.inc(sem, 16)

    def collective(self, kind, op, groups, in_ap, out_ap):
        self._ncoll += 1
        csem = self.sem(f"cc{self._ncoll}")
        h = self.p.collective_compute(
            kind, op, replica_groups=groups, ins=[in_ap.opt()], outs=[out_ap.opt()]
        )
        h.rec[2].append((csem, None))
        self.semctr[csem] = 1
        return (csem, 1)

    def wait_all_dma(self):
        for e in ("q", "p", "s"):
            ds = self.dma_sems[e]
            if self.semctr.get(ds, 0):
                getattr(self, e).wait(ds, self.semctr[ds])

    def barrier(self):
        self.phase_idx += 1
        self.wait_all_dma()
        for e in ENGS:
            ps = self.phase_sems[e]
            if e in ("v", "p"):
                h = getattr(self, e).engine_nop()
            elif e == "s":
                h = self.s.activation(self.bar_dummy[0:1, :], self.bar_dummy[0:1, :], AF.Copy)
            else:
                h = getattr(self, e).nop()
            h.inc(ps, 1)
        for e in ENGS:
            for f in ENGS:
                if f != e:
                    getattr(self, e).wait(self.phase_sems[f], self.phase_idx)
        self.bank_tok = [None] * 8

    def replay(self, engname, eng):
        for rec in self.streams[engname]:
            kind = rec[0]
            if kind == "wait":
                sem, cnt = rec[1]
                eng.wait_ge(sem, cnt)
            else:
                fn, args, kwargs = rec[1]
                h = getattr(eng, fn)(*args, **kwargs)
                for sem, n in rec[2]:
                    if n is None:
                        h.then_inc(sem)
                    else:
                        h.then_inc(sem, n)

    def finalize(self):
        with self.nc.Block() as block:
            block.tensor(lambda eng: self.replay("t", eng))
            block.vector(lambda eng: self.replay("v", eng))
            block.scalar(lambda eng: self.replay("s", eng))
            block.gpsimd(lambda eng: self.replay("p", eng))
            block.sync(lambda eng: self.replay("q", eng))


def init_consts(c):
    d_if = c.dram("c_ident_f", [128, 128], F32, kind="ExternalInput")
    c.dma("q", c.ident_f[:], d_if)
    c.v.memset(c.ones_bc[:], 1.0)
    c.v.memset(c.onesb_col[:], 1.0)
    c.v.memset(c.bar_dummy[:], 0.0)
    for v in (1e-5, -0.5, -1.0, 1.0, 0.5, 0.001):
        c.cc(v)


def const_inmaps():
    return {"c_ident_f": np.eye(128, dtype=np.float32)}


def load_weight_bf16(c, dram_ap, name, K, M, queue="p"):
    kc = (K + 127) // 128
    sb = c.ab(name, kc * M)
    for k in range(kc):
        rows = min(128, K - k * 128)
        c.dma(queue, sb[0:rows, k * M:(k + 1) * M],
              dram_ap[k * 128:k * 128 + rows, :])

    def sl(k, m0, mlen):
        return sb[:, k * M + m0: k * M + m0 + mlen]

    return sb, sl


def load_f32(c, dram_ap, name, rows, cols, queue="q"):
    sb = c.af(name, cols)
    c.dma(queue, sb[0:rows, 0:cols], dram_ap)
    return sb


def bcast_row_to(c, row_ap_fn, out_ap_fn, n_tiles, banks=(6, 7), evac_eng="s",
                 kdim=128):
    ms = c.sem("mm")
    tok = None
    for n in range(n_tiles):
        b = banks[n % len(banks)]
        if c.bank_tok[b] is not None:
            c.t.wait_tok(c.bank_tok[b])
        h = c.t.matmul(c.psum[b][0:kdim, 0:512], c.ones_bc[0:1, 0:kdim],
                       row_ap_fn(n), start=True, stop=True)
        mt = h.inc(ms)
        eng = getattr(c, evac_eng)
        eng.wait_tok(mt)
        if evac_eng == "s":
            h2 = eng.activation(out_ap_fn(n), c.psum[b][0:kdim, 0:512], AF.Copy)
        else:
            h2 = eng.tensor_copy(out_ap_fn(n), c.psum[b][0:kdim, 0:512])
        tok = h2.inc(c.sem("ev_" + evac_eng))
        c.bank_tok[b] = tok
    return tok


def softplus_inplace(c, ap, tmp_ap, wait_tok=None):
    if wait_tok is not None:
        c.s.wait_tok(wait_tok)
    c.s.activation(tmp_ap, ap, AF.Abs)
    c.s.activation(tmp_ap, tmp_ap, AF.Exp, scale=c.cc(-1.0))
    h = c.s.activation(tmp_ap, tmp_ap, AF.Ln, bias=c.cc(1.0))
    st = h.inc(c.sem("ev_s"))
    c.v.wait_tok(st)
    return c.v.scalar_tensor_tensor(ap, ap, 0.0, tmp_ap, ALU.max, ALU.add)



def seg_max(c, x3, wbuf3, mseg, nch, E):
    w = E // 2
    h = c.v.tensor_tensor(wbuf3[:, :, 0:w], x3[:, :, 0:w], x3[:, :, w:2 * w], ALU.max)
    w //= 2
    while w >= 1:
        h = c.v.tensor_tensor(wbuf3[:, :, 0:w], wbuf3[:, :, 0:w], wbuf3[:, :, w:2 * w], ALU.max)
        w //= 2
    h = c.v.tensor_tensor(mseg.rearrange("p (c e) -> p c e", e=1),
                          wbuf3[:, :, 0:1], wbuf3[:, :, 0:1], ALU.max)
    return h


def batched_topk(c, val, msk, scr, dbuf, wbuf, mseg, nch, E, kk, neg30):
    ev = c.sem("ev_v")
    es = c.sem("ev_s")
    x3 = lambda t: t.rearrange("p (c e) -> p c e", e=E)
    m3 = mseg.rearrange("p (c e) -> p c e", e=1)
    c.v.scalar_tensor_tensor(scr, val, 0.0, val, ALU.add, ALU.bypass)
    for i in range(kk):
        seg_max(c, x3(scr), wbuf.rearrange("p (c e) -> p c e", e=E // 2), mseg, nch, E)
        a1, a2 = bass.broadcast_tensor_aps(x3(scr), m3)
        h = c.v.tensor_tensor(x3(dbuf), a1, a2, ALU.subtract)
        vt = h.inc(ev)
        c.s.wait_tok(vt)
        h = c.s.activation(dbuf, dbuf, AF.Sign)
        st = h.inc(es)
        c.v.wait_tok(st)
        if i == 0:
            h = c.v.scalar_tensor_tensor(msk, dbuf, 0.0, dbuf, ALU.add, ALU.bypass)
        else:
            h = c.v.tensor_tensor(msk, msk, dbuf, ALU.max)
        if i < kk - 1:
            c.v.scalar_tensor_tensor(dbuf, dbuf, 1.0, neg30, ALU.add, ALU.mult)
            h = c.v.tensor_tensor(scr, scr, dbuf, ALU.add)
    h = c.v.scalar_tensor_tensor(msk, msk, 1.0, msk, ALU.add, ALU.bypass)
    return h


def emit_layernorm(c, pre, x_tile, xn_tile_b, gamma_col=None, beta_col=None,
                   eps=1e-5, x_is_bf16=False, xn_tile_f=None, f32_coeffs=False):
    ms = c.sem("mm")
    ev = c.sem("ev_v")
    es = c.sem("ev_s")
    xb_own = not x_is_bf16
    if xb_own:
        xb = c.ab(pre + "_xb", DC * TOK)
    xsq = c.ab(pre + "_xsq", 2 * TOK)
    if f32_coeffs:
        ab2 = c.af(pre + "_ab", 2 * TOK)
        tmp = c.af(pre + "_tmp", TOK)
    else:
        ab2 = c.ab(pre + "_ab", 2 * TOK)
        tmp = c.ab(pre + "_tmp", TOK)
    rowsf = c.af(pre + "_rows", 2 * TOK + 2 * TOK)
    rows = rowsf[0:1, 0:2 * TOK]
    arow = rowsf[0:1, 2 * TOK:3 * TOK]
    brow = rowsf[0:1, 3 * TOK:4 * TOK]
    onesb = c.onesb_col

    sq_free = [None, None]
    for j in range(DC):
        if xb_own:
            hcp = c.v.tensor_copy(xb[:, j * TOK:(j + 1) * TOK], x_tile(j))
            ct = hcp.inc(ev)
            src_b = xb[:, j * TOK:(j + 1) * TOK]
        else:
            ct = None
            src_b = x_tile(j)
        jj = j % 2
        if sq_free[jj] is not None:
            c.s.wait_tok(sq_free[jj])
        hsq = c.s.activation(xsq[:, jj * TOK:(jj + 1) * TOK], x_tile(j), AF.Square)
        st = hsq.inc(es)
        if ct is not None:
            c.t.wait_tok(ct)
        c.t.wait_tok(st)
        h = None
        for n in range(NT):
            c.t.matmul(c.psum[n][0:1, 0:512], onesb[:],
                       src_b[:, n * 512:n * 512 + 512],
                       start=(j == 0), stop=(j == DC - 1))
            h = c.t.matmul(c.psum[4 + n][0:1, 0:512], onesb[:],
                           xsq[:, jj * TOK + n * 512: jj * TOK + n * 512 + 512],
                           start=(j == 0), stop=(j == DC - 1))
        sq_free[jj] = h.inc(ms)
    mm_done = sq_free[(DC - 1) % 2]
    c.v.wait_tok(mm_done)
    for n in range(NT):
        c.v.tensor_copy(rows[0:1, n * 512:(n + 1) * 512], c.psum[n][0:1, 0:512])
        c.v.tensor_copy(rows[0:1, TOK + n * 512: TOK + n * 512 + 512],
                        c.psum[4 + n][0:1, 0:512])
    c.v.tensor_scalar(rows[0:1, 0:TOK], rows[0:1, 0:TOK], 1.0 / D, None, ALU.mult)
    c.v.tensor_scalar(rows[0:1, TOK:2 * TOK], rows[0:1, TOK:2 * TOK], 1.0 / D, None, ALU.mult)
    c.v.tensor_tensor(arow[0:1, :], rows[0:1, 0:TOK], rows[0:1, 0:TOK], ALU.mult)
    h = c.v.tensor_tensor(arow[0:1, :], rows[0:1, TOK:2 * TOK], arow[0:1, :], ALU.subtract)
    vt = h.inc(ev)
    c.s.wait_tok(vt)
    c.s.activation(arow[0:1, :], arow[0:1, :], AF.Ln, bias=c.cc(eps, 1))
    h = c.s.activation(arow[0:1, :], arow[0:1, :], AF.Exp, scale=c.cc(-0.5, 1))
    st = h.inc(es)
    c.v.wait_tok(st)
    c.v.tensor_tensor(brow[0:1, :], rows[0:1, 0:TOK], arow[0:1, :], ALU.mult)
    h = c.v.tensor_scalar(brow[0:1, :], brow[0:1, :], -1.0, None, ALU.mult)
    vt = h.inc(ev)
    c.t.wait_tok(vt)
    bcast_row_to(c, lambda n: arow[0:1, n * 512:(n + 1) * 512],
                 lambda n: ab2[:, n * 512:(n + 1) * 512], NT, banks=(0, 1))
    btok = bcast_row_to(c, lambda n: brow[0:1, n * 512:(n + 1) * 512],
                        lambda n: ab2[:, TOK + n * 512: TOK + n * 512 + 512], NT,
                        banks=(2, 3))
    A_ap = ab2[:, 0:TOK]
    B_ap = ab2[:, TOK:2 * TOK]
    c.v.wait_tok(btok)
    last = None
    for j in range(DC):
        c.v.tensor_tensor(tmp[:], x_tile(j), A_ap, ALU.mult)
        if gamma_col is None:
            last = c.v.tensor_tensor(xn_tile_b(j), tmp[:], B_ap, ALU.add)
            if xn_tile_f is not None:
                last = c.v.tensor_tensor(xn_tile_f(j), tmp[:], B_ap, ALU.add)
        else:
            c.v.tensor_tensor(tmp[:], tmp[:], B_ap, ALU.add)
            last = c.v.tensor_scalar(xn_tile_b(j), tmp[:], gamma_col(j), beta_col(j),
                                     ALU.mult, ALU.add)
            if xn_tile_f is not None:
                last = c.v.tensor_scalar(xn_tile_f(j), tmp[:], gamma_col(j), beta_col(j),
                                         ALU.mult, ALU.add)
    tok = last.inc(ev)
    if xb_own:
        c.fb(pre + "_xb")
    c.fb(pre + "_xsq")
    if f32_coeffs:
        c.ff(pre + "_ab", pre + "_tmp")
    else:
        c.fb(pre + "_ab", pre + "_tmp")
    c.ff(pre + "_rows")
    return tok


def gemm(c, nK, nM, nN, lhsT, rhs, evac, banks=(0, 1, 2, 3), nfree=512):
    ms = c.sem("mm")
    for m in range(nM):
        for n in range(nN):
            b = banks[(m * nN + n) % len(banks)]
            if c.bank_tok[b] is not None:
                c.t.wait_tok(c.bank_tok[b])
            h = None
            for k in range(nK):
                h = c.t.matmul(c.psum[b][:, 0:nfree], lhsT(k, m), rhs(k, n),
                               start=(k == 0), stop=(k == nK - 1))
            mtk = h.inc(ms)
            c.bank_tok[b] = evac(m, n, c.psum[b], mtk)


DEBUG_B0 = False


def emit_block0(c, x_in, x_out):
    dbg = c.dram("dbg", [128, 8192], F32, kind="ExternalOutput") if DEBUG_B0 else None
    ev = c.sem("ev_v")
    es = c.sem("ev_s")
    ms = c.sem("mm")

    xn = c.ab("b0_xn", DC * TOK)

    def xn_sl(k, n0, nlen):
        return xn[:, k * TOK + n0: k * TOK + n0 + nlen]

    x_f = c.af("b0_x", 16384)
    for j in range(DC):
        c.dma("q", x_f[:, j * TOK:(j + 1) * TOK], x_in[j * 128:(j + 1) * 128, :])
    c.barrier()
    emit_layernorm(c, "b0ln", lambda j: x_f[:, j * TOK:(j + 1) * TOK],
                   lambda j: xn[:, j * TOK:(j + 1) * TOK])
    c.ff("b0_x")
    c.barrier()

    scratch = c.af("b0_scr", 2048)
    ut = scratch[:, 0:512]
    mt = scratch[:, 512:1024]
    rt = scratch[:, 1024:1536]
    bbf = c.af("b0_bb", 2 * D)
    rowf = c.af("b0_row", 2 * D)
    kphi = c.ab("b0_kphi", 16 * D)
    bkr = c.dram("b0_bk", [1, D], F32, kind="ExternalInput")
    bvr = c.dram("b0_bv", [1, D], F32, kind="ExternalInput")
    c.dma("q", rowf[0:1, 0:D], bkr)
    c.dma("q", rowf[0:1, D:2 * D], bvr)
    wk, wk_sl = load_weight_bf16(c, c.dram("b0_WkT", [D, D], F32, kind="ExternalInput"), "b0_wk", D, D)
    c.barrier()
    bcast_row_to(c, lambda n: rowf[0:1, n * 512:(n + 1) * 512],
                 lambda n: bbf[:, n * 512:(n + 1) * 512], 2 * D // 512)
    c.barrier()

    def proj_featuremap(dst, wsl, bias_fn, token_major):
        outer = 16 if token_major else DC
        inner = 2 if token_major else NT
        for o in range(outer):
            for n in range(inner):
                b = (o * inner + n) % 4
                if c.bank_tok[b] is not None:
                    c.t.wait_tok(c.bank_tok[b])
                h = None
                for k in range(DC):
                    if token_major:
                        h = c.t.matmul(c.psum[b][:, 0:512],
                                       xn[:, k * TOK + o * 128: k * TOK + o * 128 + 128],
                                       wsl(k, n * 512, 512),
                                       start=(k == 0), stop=(k == DC - 1))
                    else:
                        h = c.t.matmul(c.psum[b][:, 0:512],
                                       wsl(k, o * 128, 128),
                                       xn_sl(k, n * 512, 512),
                                       start=(k == 0), stop=(k == DC - 1))
                mtk = h.inc(ms)
                c.v.wait_tok(mtk)
                bias_fn(o, n, b)
                h = c.v.tensor_scalar(mt[:], ut[:], 0.0, None, ALU.min)
                vt1 = h.inc(ev)
                c.bank_tok[b] = vt1
                c.s.wait_tok(vt1)
                h = c.s.activation(mt[:], mt[:], AF.Exp)
                st1 = h.inc(es)
                c.v.tensor_scalar(rt[:], ut[:], 0.0, None, ALU.max)
                c.v.wait_tok(st1)
                if token_major:
                    dslice = dst[:, o * D + n * 512: o * D + n * 512 + 512]
                else:
                    dslice = dst[:, o * TOK + n * 512: o * TOK + n * 512 + 512]
                h = c.v.tensor_tensor(dslice, mt[:], rt[:], ALU.add)
                h.inc(ev)

    proj_featuremap(kphi, wk_sl,
                    lambda o, n, b: c.v.tensor_tensor(
                        ut[:], c.psum[b][:, 0:512], bbf[:, n * 512:(n + 1) * 512], ALU.add),
                    token_major=True)
    c.barrier()
    c.fb("b0_wk")
    wv, wv_sl = load_weight_bf16(c, c.dram("b0_WvT", [D, D], F32, kind="ExternalInput"), "b0_wv", D, D)
    vtm = c.ab("b0_vtm", 16 * 512)
    kvpack = c.af("b0_kvpack", 544)
    c.barrier()
    for hh in range(2):
        for t in range(16):
            b = t % 4
            if c.bank_tok[b] is not None:
                c.t.wait_tok(c.bank_tok[b])
            h = None
            for k in range(DC):
                h = c.t.matmul(c.psum[b][:, 0:512],
                               xn[:, k * TOK + t * 128: k * TOK + t * 128 + 128],
                               wv_sl(k, hh * 512, 512), start=(k == 0), stop=(k == DC - 1))
            mtk = h.inc(ms)
            c.v.wait_tok(mtk)
            h2 = c.v.tensor_tensor(vtm[:, t * 512: t * 512 + 512],
                                   c.psum[b][:, 0:512],
                                   bbf[:, D + hh * 512: D + hh * 512 + 512], ALU.add)
            c.bank_tok[b] = h2.inc(ev)
        for hl in range(8):
            h16 = hh * 8 + hl
            b = 4 + hl % 2
            if c.bank_tok[b] is not None:
                c.t.wait_tok(c.bank_tok[b])
            h = None
            for t in range(16):
                h = c.t.matmul(c.psum[b][0:64, 0:64],
                               kphi[:, t * D + h16 * 64: t * D + h16 * 64 + 64],
                               vtm[:, t * 512 + hl * 64: t * 512 + hl * 64 + 64],
                               start=(t == 0), stop=(t == 15))
            mtk = h.inc(ms)
            pbase = (h16 % 2) * 64
            c.v.wait_tok(mtk)
            h2 = c.v.tensor_copy(kvpack[pbase:pbase + 64, (h16 // 2) * 64:(h16 // 2) * 64 + 64],
                                 c.psum[b][0:64, 0:64])
            c.bank_tok[b] = h2.inc(ev)
        c.barrier()
    c.fb("b0_wv", "b0_vtm")
    for j in range(DC):
        b = 2 + j % 2
        if c.bank_tok[b] is not None:
            c.t.wait_tok(c.bank_tok[b])
        h = None
        for t in range(16):
            h = c.t.matmul(c.psum[b][:, 0:1],
                           kphi[:, t * D + j * 128: t * D + j * 128 + 128],
                           c.onesb_col[:], start=(t == 0), stop=(t == 15))
        mtk = h.inc(ms)
        c.v.wait_tok(mtk)
        h2 = c.v.tensor_copy(kvpack[:, 512 + j:513 + j], c.psum[b][:, 0:1])
        c.bank_tok[b] = h2.inc(ev)
    c.barrier()
    c.fb("b0_kphi")
    kv_in = c.dram("b0_kvin", [128, 544], F32)
    kv_out = c.dram("b0_kvout", [128, 544], F32)
    c.dma("p", kv_in, kvpack[:, 0:544])
    c.p.wait(c.dma_sems["p"], c.semctr[c.dma_sems["p"]])
    ctok = c.collective("AllReduce", ALU.add, GROUPS_BATCH, kv_in, kv_out)
    c.p.wait_tok(ctok)
    c.dma("p", kvpack[:, 0:544], kv_out)
    wq, wq_sl = load_weight_bf16(c, c.dram("b0_WqT", [D, D], F32, kind="ExternalInput"), "b0_wq", D, D)
    bq = load_f32(c, c.dram("b0_bqc", [128, DC], F32, kind="ExternalInput"), "b0_bq", 128, DC)
    phq = c.ab("b0_phq", DC * TOK)
    c.barrier()
    proj_featuremap(phq, wq_sl,
                    lambda m, n, b: c.v.tensor_scalar(
                        ut[:], c.psum[b][:, 0:512], bq[:, m:m + 1], None, ALU.add),
                    token_major=False)
    c.barrier()
    c.fb("b0_wq")
    attn_t = [c.ab(f"b0_attn{j}", TOK) for j in range(DC)]
    kvb = c.ab("b0_kvb", 512)
    ksb = c.ab("b0_ksb", 8)
    c.v.tensor_copy(kvb[:], kvpack[:, 0:512])
    h = c.v.tensor_copy(ksb[:], kvpack[:, 512:520])
    h.inc(ev)
    nr = scratch[0:1, 1536:2048]
    rnb = c.af("b0_rnb", 512)
    c.barrier()
    for h16 in range(16):
        pbase = (h16 % 2) * 64
        j = h16 // 2
        for n in range(NT):
            b2 = 4 + (h16 * NT + n) % 2
            b3 = 6 + (h16 * NT + n) % 2
            b = (h16 * NT + n) % 4
            if c.bank_tok[b2] is not None:
                c.t.wait_tok(c.bank_tok[b2])
            h = c.t.matmul(c.psum[b2][0:1, 0:512],
                           ksb[pbase:pbase + 64, j:j + 1],
                           phq[pbase:pbase + 64, j * TOK + n * 512: j * TOK + n * 512 + 512],
                           start=True, stop=True)
            ntk = h.inc(ms)
            c.v.wait_tok(ntk)
            c.v.tensor_scalar(nr[0:1, :], c.psum[b2][0:1, 0:512], 1e-6, None, ALU.add)
            h2 = c.v.reciprocal(nr[0:1, :], nr[0:1, :])
            rtok = h2.inc(ev)
            c.bank_tok[b2] = rtok
            if c.bank_tok[b3] is not None:
                c.t.wait_tok(c.bank_tok[b3])
            c.t.wait_tok(rtok)
            h = c.t.matmul(c.psum[b3][0:64, 0:512], c.ones_bc[0:1, 0:64],
                           nr[0:1, :], start=True, stop=True)
            btk = h.inc(ms)
            c.v.wait_tok(btk)
            h2 = c.v.tensor_copy(rnb[0:64, :], c.psum[b3][0:64, 0:512])
            rbt = h2.inc(ev)
            c.bank_tok[b3] = rbt
            if c.bank_tok[b] is not None:
                c.t.wait_tok(c.bank_tok[b])
            h = c.t.matmul(c.psum[b][0:64, 0:512],
                           kvb[pbase:pbase + 64, j * 64: j * 64 + 64],
                           phq[pbase:pbase + 64, j * TOK + n * 512: j * TOK + n * 512 + 512],
                           start=True, stop=True)
            atk = h.inc(ms)
            c.v.wait_tok(atk)
            h2 = c.v.tensor_tensor(
                attn_t[j][pbase:pbase + 64, n * 512: n * 512 + 512],
                c.psum[b][0:64, 0:512], rnb[0:64, :], ALU.mult)
            c.bank_tok[b] = h2.inc(ev)
    c.barrier()
    c.fb("b0_phq", "b0_kvb", "b0_ksb")
    c.ff("b0_kvpack", "b0_rnb", "b0_bb", "b0_row")
    wo, wo_sl = load_weight_bf16(c, c.dram("b0_WoT", [D, D], F32, kind="ExternalInput"), "b0_wo", D, D)
    bo = load_f32(c, c.dram("b0_boc", [128, DC], F32, kind="ExternalInput"), "b0_bo", 128, DC)
    attn0 = c.dram("b0_attn0", [D, TOK], BF16)
    aout = c.ab("b0_aout", TOK)
    c.barrier()
    for m in range(DC):
        tok = None
        for n in range(NT):
            b = n % 4
            if c.bank_tok[b] is not None:
                c.t.wait_tok(c.bank_tok[b])
            h = None
            for k in range(DC):
                h = c.t.matmul(c.psum[b][:, 0:512], wo_sl(k, m * 128, 128),
                               attn_t[k][:, n * 512: n * 512 + 512],
                               start=(k == 0), stop=(k == DC - 1))
            mtk = h.inc(ms)
            c.v.wait_tok(mtk)
            h2 = c.v.tensor_scalar(aout[:, n * 512:(n + 1) * 512], c.psum[b][:, 0:512],
                                   bo[:, m:m + 1], None, ALU.add)
            tok = h2.inc(ev)
            c.bank_tok[b] = tok
        c.q.wait_tok(tok)
        c.dma("q", attn0[m * 128:(m + 1) * 128, :], aout[:])
        c.v.wait(c.dma_sems["q"], c.semctr[c.dma_sems["q"]])
    c.barrier()
    c.fb(*[f"b0_attn{j}" for j in range(DC)], "b0_wo", "b0_aout")
    c.ff("b0_bq", "b0_bo")

    acc = c.af("b0_acc", 16384)
    sw1, sw1_sl = load_weight_bf16(c, c.dram("b0_sW1", [D, 512], F32, kind="ExternalInput"), "b0_sw1", D, 512)
    sb1 = load_f32(c, c.dram("b0_sb1c", [128, 4], F32, kind="ExternalInput"), "b0_sb1", 128, 4)
    hsh = c.ab("b0_hsh", 4 * TOK)
    c.barrier()

    def ev_silu(dst_fn, bias, m, n, ps, mtk):
        c.s.wait_tok(mtk)
        h2 = c.s.activation(dst_fn(m, n), ps[:, 0:512], AF.Silu, bias=bias(m))
        return h2.inc(es)

    gemm(c, DC, 4, NT,
         lambda k, m: sw1_sl(k, m * 128, 128),
         lambda k, n: xn_sl(k, n * 512, 512),
         lambda m, n, ps, mtk: ev_silu(
             lambda m2, n2: hsh[:, m2 * TOK + n2 * 512: m2 * TOK + n2 * 512 + 512],
             lambda m2: sb1[:, m2:m2 + 1], m, n, ps, mtk))
    c.barrier()
    c.fb("b0_sw1")
    sw2, sw2_sl = load_weight_bf16(c, c.dram("b0_sW2", [512, D], F32, kind="ExternalInput"), "b0_sw2", 512, D)
    sb2 = load_f32(c, c.dram("b0_sb2c", [128, DC], F32, kind="ExternalInput"), "b0_sb2", 128, DC)
    c.barrier()

    def ev_bias_to_acc(m, n, ps, mtk):
        c.v.wait_tok(mtk)
        h2 = c.v.tensor_scalar(acc[:, m * TOK + n * 512: m * TOK + n * 512 + 512],
                               ps[:, 0:512], sb2[:, m:m + 1], None, ALU.add)
        return h2.inc(ev)

    gemm(c, 4, DC, NT,
         lambda k, m: sw2_sl(k, m * 128, 128),
         lambda k, n: hsh[:, k * TOK + n * 512: k * TOK + n * 512 + 512],
         ev_bias_to_acc)
    c.barrier()
    c.fb("b0_sw2", "b0_hsh")
    c.ff("b0_sb1")
    rw, rw_sl = load_weight_bf16(c, c.dram("b0_rWT", [D, 8], F32, kind="ExternalInput"), "b0_rw", DC * 128, 8)
    ebt = load_f32(c, c.dram("b0_ebias", [128, 8], F32, kind="ExternalInput"), "b0_ebt", 128, 8)
    cwf = c.af("b0_cw", 128 * 5 + 64 + 16 + 128)
    cw = cwf[:, 0:128]
    msk = cwf[:, 128:256]
    scr = cwf[:, 256:384]
    dbuf = cwf[:, 384:512]
    sgt = cwf[:, 512:640]
    wbuf = cwf[:, 640:704]
    mseg = cwf[:, 704:720]
    neg30 = cwf[:, 720:848]
    c.v.memset(neg30, -1e30)
    c.barrier()
    ebt3 = ebt[:, 0:8].rearrange("p (c e) -> p c e", e=8)
    for t in range(16):
        b = t % 4
        if c.bank_tok[b] is not None:
            c.t.wait_tok(c.bank_tok[b])
        h = None
        for k in range(DC):
            h = c.t.matmul(c.psum[b][:, 0:8],
                           xn[:, k * TOK + t * 128: k * TOK + t * 128 + 128],
                           rw_sl(k, 0, 8), start=(k == 0), stop=(k == DC - 1))
        mtk = h.inc(ms)
        c.v.wait_tok(mtk)
        h2 = c.v.tensor_tensor(cw[:, t * 8:(t + 1) * 8], c.psum[b][:, 0:8],
                               ebt[:, 0:8], ALU.add)
        c.bank_tok[b] = h2.inc(ev)
    c.barrier()
    h2 = softplus_inplace(c, cw, sgt)
    vt = h2.inc(ev)
    c.s.wait_tok(vt)
    c.s.activation(cw, cw, AF.Ln)
    h2 = c.s.activation(cw, cw, AF.Exp, scale=c.cc(0.5))
    st = h2.inc(es)
    c.v.wait_tok(st)
    batched_topk(c, cw, msk, scr, dbuf, wbuf, mseg, 16, 8, 2, neg30)
    h2 = c.v.tensor_tensor(cw, cw, msk, ALU.mult)
    h2.inc(ev)
    c.barrier()
    c.fb("b0_rw")
    cwfm = c.af("b0_cwfm", TOK)
    cwfmb = c.ab("b0_cwfmb", TOK)
    for t in range(16):
        b = t % 4
        if c.bank_tok[b] is not None:
            c.t.wait_tok(c.bank_tok[b])
        h = c.t.transpose(c.psum[b][0:8, 0:128], cw[:, t * 8:(t + 1) * 8], c.ident_f[:])
        mtk = h.inc(ms)
        c.v.wait_tok(mtk)
        h2 = c.v.tensor_copy(cwfm[0:8, t * 128:(t + 1) * 128], c.psum[b][0:8, 0:128])
        c.bank_tok[b] = h2.inc(ev)
    h = c.v.tensor_copy(cwfmb[0:8, :], cwfm[0:8, :])
    h.inc(ev)
    if DEBUG_B0:
        c.barrier()
        c.dma("q", dbg[0:8, 1024:1024 + TOK], cwfm[0:8, :])
    rb2, rb2_sl = load_weight_bf16(c, c.dram("b0_rb2", [8, D], F32, kind="ExternalInput"), "b0_rb2s", 8, D)
    cwdram = c.dram("b0_cwdram", [8, TOK], F32)
    c.barrier()
    c.dma("q", cwdram, cwfm[0:8, :])

    def ev_add_acc(m, n, ps, mtk):
        c.v.wait_tok(mtk)
        h2 = c.v.tensor_tensor(acc[:, m * TOK + n * 512: m * TOK + n * 512 + 512],
                               acc[:, m * TOK + n * 512: m * TOK + n * 512 + 512],
                               ps[:, 0:512], ALU.add)
        return h2.inc(ev)

    gemm(c, 1, DC, NT,
         lambda k, m: rb2[0:8, m * 128:(m + 1) * 128],
         lambda k, n: cwfmb[0:8, n * 512:(n + 1) * 512],
         ev_add_acc)
    c.barrier()
    c.ff("b0_cw", "b0_cwfm", "b0_ebt")
    rw1d = c.dram("b0_rW1", [8 * DC * 128, 512], F32, kind="ExternalInput")
    rw2d = c.dram("b0_rW2", [8 * 4 * 128, D], F32, kind="ExternalInput")
    rb1d = c.dram("b0_rb1c", [8 * 128, 4], F32, kind="ExternalInput")
    he = c.ab("b0_he", 4 * TOK)
    w1e = c.ab("b0_w1e", DC * 512)
    w2e = c.ab("b0_w2e", 4 * D)
    cwbf = c.af("b0_cwb", TOK + TOK + 4)
    cwrow = cwbf[0:1, 0:TOK]
    cwb = cwbf[:, TOK:2 * TOK]
    rb1e = cwbf[:, 2 * TOK:2 * TOK + 4]
    for e in range(8):
        for k in range(DC):
            c.dma("p", w1e[:, k * 512:(k + 1) * 512],
                  rw1d[(e * DC + k) * 128:(e * DC + k + 1) * 128, :])
        for k in range(4):
            c.dma("p", w2e[:, k * D:(k + 1) * D],
                  rw2d[(e * 4 + k) * 128:(e * 4 + k + 1) * 128, :])
        c.dma("q", rb1e, rb1d[e * 128:(e + 1) * 128, :])
        c.dma("q", cwrow, cwdram[e:e + 1, :])
        c.barrier()
        bcast_row_to(c, lambda n: cwrow[0:1, n * 512:(n + 1) * 512],
                     lambda n: cwb[:, n * 512:(n + 1) * 512], NT, banks=(6, 7),
                     evac_eng="v")

        def ev_he(m, n, ps, mtk):
            c.s.wait_tok(mtk)
            h2 = c.s.activation(ut[:], ps[:, 0:512], AF.Silu, bias=rb1e[:, m:m + 1])
            stok = h2.inc(es)
            c.v.wait_tok(stok)
            h2 = c.v.tensor_tensor(he[:, m * TOK + n * 512: m * TOK + n * 512 + 512],
                                   ut[:], cwb[:, n * 512:(n + 1) * 512], ALU.mult)
            h2.inc(ev)
            return stok

        gemm(c, DC, 4, NT,
             lambda k, m: w1e[:, k * 512 + m * 128: k * 512 + m * 128 + 128],
             lambda k, n: xn_sl(k, n * 512, 512), ev_he)
        c.barrier()
        if DEBUG_B0 and e == 0:
            c.dma("q", dbg[:, 3072:3072 + TOK], cwb[:])
            c.dma("p", dbg[:, 5120:5120 + TOK], he[:, 0:TOK])
            c.barrier()
        gemm(c, 4, DC, NT,
             lambda k, m: w2e[:, k * D + m * 128: k * D + m * 128 + 128],
             lambda k, n: he[:, k * TOK + n * 512: k * TOK + n * 512 + 512],
             ev_add_acc)
        c.barrier()
    c.fb("b0_he", "b0_w1e", "b0_w2e", "b0_cwfmb", "b0_rb2s")
    c.ff("b0_cwb")

    xj = c.af("b0_xj", TOK)
    af2 = c.af("b0_af", TOK)
    for j in range(DC):
        t1tok = c.dma("q", xj[:], x_in[j * 128:(j + 1) * 128, :])
        t2tok = c.dma("p", af2[:], attn0[j * 128:(j + 1) * 128, :])
        c.v.wait_tok(t1tok)
        c.v.wait_tok(t2tok)
        c.v.tensor_tensor(xj[:], xj[:], acc[:, j * TOK:(j + 1) * TOK], ALU.add)
        h = c.v.tensor_tensor(xj[:], xj[:], af2[:], ALU.add)
        vt = h.inc(ev)
        c.q.wait_tok(vt)
        c.dma("q", x_out[j * 128:(j + 1) * 128, :], xj[:])
        c.barrier()
    c.ff("b0_xj", "b0_af", "b0_acc", "b0_scr")
    c.fb("b0_xn")
    c.barrier()


def prep_block0(inputs):
    A = np.ascontiguousarray
    d = {}
    d["b0_WqT"] = A(inputs["L0_attn_Wq"].T)
    d["b0_WkT"] = A(inputs["L0_attn_Wk"].T)
    d["b0_WvT"] = A(inputs["L0_attn_Wv"].T)
    d["b0_WoT"] = A(inputs["L0_attn_Wo"].T)
    d["b0_bqc"] = A(inputs["L0_attn_bq"].reshape(DC, 128).T)
    d["b0_bk"] = A(inputs["L0_attn_bk"].reshape(1, D))
    d["b0_bv"] = A(inputs["L0_attn_bv"].reshape(1, D))
    d["b0_boc"] = A(inputs["L0_attn_bo"].reshape(DC, 128).T)
    d["b0_sW1"] = A(inputs["L0_ffn_sW1"])
    d["b0_sW2"] = A(inputs["L0_ffn_sW2"])
    d["b0_sb1c"] = A(inputs["L0_ffn_sb1"].reshape(4, 128).T)
    d["b0_sb2c"] = A(inputs["L0_ffn_sb2"].reshape(DC, 128).T)
    d["b0_rWT"] = A(inputs["L0_ffn_routing_W"].T)
    d["b0_ebias"] = A(np.broadcast_to(inputs["L0_ffn_expert_bias"], (128, 8)))
    d["b0_rW1"] = A(inputs["L0_ffn_rW1"].reshape(8 * DC * 128, 512))
    d["b0_rW2"] = A(inputs["L0_ffn_rW2"].reshape(8 * 4 * 128, D))
    d["b0_rb1c"] = A(inputs["L0_ffn_rb1"].reshape(8, 4, 128).transpose(0, 2, 1).reshape(8 * 128, 4))
    d["b0_rb2"] = A(inputs["L0_ffn_rb2"])
    return d


def x_shard_fm(x_full, core):
    flat = np.asarray(x_full, np.float32).reshape(2 * SEQ, D)
    sl = flat[core * TOK:(core + 1) * TOK, :]
    return np.ascontiguousarray(sl.T)


def build_block0():
    nc = bass.Bass("TRN2", target_bir_lowering=False)
    with contextlib.ExitStack() as stack:
        c = Ctx(nc, stack)
        x_in = c.dram("x_in", [D, TOK], F32, kind="ExternalInput")
        x_out = c.dram("x_out", [D, TOK], F32, kind="ExternalOutput")
        init_consts(c)
        emit_block0(c, x_in, x_out)
        c.finalize()
    return nc



def build_block2(upto=99):
    nc = bass.Bass("TRN2", target_bir_lowering=False)
    with contextlib.ExitStack() as stack:
        c = Ctx(nc, stack, ab_cols=43520, af_cols=30720)
        x_in = c.dram("x_in", [D, TOK], F32, kind="ExternalInput")
        x_out = c.dram("x_out", [D, TOK], F32, kind="ExternalOutput")
        init_consts(c)
        emit_block2(c, x_in, x_out, upto=upto)
        c.finalize()
    return nc


_NC_CACHE = {}


def kernel(**inputs):
    x = np.asarray(inputs["L0_attn_query"], np.float32)
    shards = [x_shard_fm(x, core) for core in range(NCORES)]
    consts = const_inmaps()
    stages = [
        ("b0", build_block0, lambda: [prep_block0(inputs)] * NCORES),
        ("b1", build_block1, lambda: [prep_block1(inputs, core) for core in range(NCORES)]),
        ("b2", build_block2, lambda: [prep_block2(inputs, core) for core in range(NCORES)]),
        ("b3", build_block3, lambda: [prep_block3(inputs, core) for core in range(NCORES)]),
    ]
    for name, builder, prepf in stages:
        if name not in _NC_CACHE:
            _NC_CACHE[name] = builder()
        nc = _NC_CACHE[name]
        preps = prepf()
        in_maps = []
        for core in range(NCORES):
            m = {"x_in": shards[core]}
            m.update(consts)
            m.update(preps[core])
            in_maps.append(m)
        res = run_bass_kernel_spmd(nc, in_maps, core_ids=list(range(NCORES)))
        shards = [np.ascontiguousarray(res.results[i]["x_out"]) for i in range(NCORES)]
    full = np.concatenate([o.T for o in shards], axis=0)
    return full.reshape(2, SEQ, D)



def _b2_passthrough(c, x_in, x_out):
    tmp = c.af("b2_pt", TOK)
    for j in range(DC):
        t = c.dma("q", tmp[:], x_in[j * 128:(j + 1) * 128, :])
        c.q.wait_tok(t)
        c.dma("q", x_out[j * 128:(j + 1) * 128, :], tmp[:])
        c.barrier()
    c.ff("b2_pt")
    c.barrier()


def emit_block2(c, x_in, x_out, rmask_name="b2_rmask", upto=99):
    ev = c.sem("ev_v")
    es = c.sem("ev_s")
    ms = c.sem("mm")
    SCALE = 1.0 / float(np.sqrt(np.float32(1024 * 2048)))

    xn = c.ab("b2_xn", DC * TOK)

    def xn_sl(k, n0, nlen):
        return xn[:, k * TOK + n0: k * TOK + n0 + nlen]

    x_f = c.af("b2_x", 16384)
    for j in range(DC):
        c.dma("q", x_f[:, j * TOK:(j + 1) * TOK], x_in[j * 128:(j + 1) * 128, :])
    c.barrier()
    lntok = emit_layernorm(c, "b2ln", lambda j: x_f[:, j * TOK:(j + 1) * TOK],
                           lambda j: xn[:, j * TOK:(j + 1) * TOK],
                           xn_tile_f=lambda j: x_f[:, j * TOK:(j + 1) * TOK],
                           f32_coeffs=True)
    xnd = c.dram("b2_xnd", [D, TOK], BF16)
    xnfd = c.dram("b2_xnfd", [D, TOK], F32)
    c.q.wait_tok(lntok)
    for j in range(DC):
        c.dma("q", xnd[j * 128:(j + 1) * 128, :], xn[:, j * TOK:(j + 1) * TOK])
        c.dma("q", xnfd[j * 128:(j + 1) * 128, :], x_f[:, j * TOK:(j + 1) * TOK])
    c.wait_all_dma()
    c.barrier()
    c.ff("b2_x")


    if upto == 1:
        _b2_passthrough(c, x_in, x_out)
        return
    vtm_d = c.dram("b2_vtmd", [TOK, D], BF16)
    ktm_d = c.dram("b2_ktmd", [TOK, D], BF16)
    kfm_d = c.dram("b2_kfmd", [D, TOK], BF16)
    h1 = c.ab("b2_h1", DC * TOK)
    stage = c.ab("b2_stage", 2048)
    brow = c.af("b2_brow", 2 * D)
    bbf = c.af("b2_bb", D)
    q_fm = None

    def mlp_h1(wname, bname):
        w1, w1_sl = load_weight_bf16(c, c.dram(wname, [D, D], F32, kind="ExternalInput"), "b2_w1x", D, D)
        b1 = load_f32(c, c.dram(bname, [128, DC], F32, kind="ExternalInput"), "b2_b1x", 128, DC)
        c.barrier()
        gemm(c, DC, DC, NT,
             lambda k, m: w1_sl(k, m * 128, 128),
             lambda k, n: xn_sl(k, n * 512, 512),
             lambda m, n, ps, mtk: (
                 c.s.wait_tok(mtk),
                 c.s.activation(h1[:, m * TOK + n * 512: m * TOK + n * 512 + 512],
                                ps[:, 0:512], AF.Gelu, bias=b1[:, m:m + 1]).inc(es))[-1])
        c.barrier()
        c.fb("b2_w1x")
        c.ff("b2_b1x")

    mlp_h1("b2_W1v", "b2_b1vc")
    w2, w2_sl = load_weight_bf16(c, c.dram("b2_W2v", [D, D], F32, kind="ExternalInput"), "b2_w2x", D, D)
    c.dma("q", brow[0:1, 0:D], c.dram("b2_b2vr", [1, D], F32, kind="ExternalInput"))
    c.barrier()
    bcast_row_to(c, lambda n: brow[0:1, n * 512:(n + 1) * 512],
                 lambda n: bbf[:, n * 512:(n + 1) * 512], 2)
    c.barrier()
    for t in range(16):
        tok = None
        for n in range(2):
            b = (t * 2 + n) % 4
            if c.bank_tok[b] is not None:
                c.t.wait_tok(c.bank_tok[b])
            h = None
            for k in range(DC):
                h = c.t.matmul(c.psum[b][:, 0:512],
                               h1[:, k * TOK + t * 128: k * TOK + t * 128 + 128],
                               w2_sl(k, n * 512, 512), start=(k == 0), stop=(k == DC - 1))
            mtk = h.inc(ms)
            c.v.wait_tok(mtk)
            h2 = c.v.tensor_tensor(stage[:, n * 512 + 0: n * 512 + 512],
                                   c.psum[b][:, 0:512], bbf[:, n * 512:(n + 1) * 512], ALU.add)
            tok = h2.inc(ev)
            c.bank_tok[b] = tok
        c.q.wait_tok(tok)
        c.dma("q", vtm_d[t * 128:(t + 1) * 128, :], stage[:, 0:1024])
        c.q.wait(c.dma_sems["q"], c.semctr[c.dma_sems["q"]])
        c.v.wait(c.dma_sems["q"], c.semctr[c.dma_sems["q"]])
    c.barrier()
    c.fb("b2_w2x")
    mlp_h1("b2_W1k", "b2_b1kc")
    w2, w2_sl = load_weight_bf16(c, c.dram("b2_W2k", [D, D], F32, kind="ExternalInput"), "b2_w2x", D, D)
    b2k = load_f32(c, c.dram("b2_b2kc", [128, DC], F32, kind="ExternalInput"), "b2_b2k", 128, DC)
    c.dma("q", brow[0:1, 0:D], c.dram("b2_b2kr", [1, D], F32, kind="ExternalInput"))
    c.barrier()
    bcast_row_to(c, lambda n: brow[0:1, n * 512:(n + 1) * 512],
                 lambda n: bbf[:, n * 512:(n + 1) * 512], 2)
    c.barrier()
    for t in range(16):
        tok = None
        for n in range(2):
            b = (t * 2 + n) % 4
            if c.bank_tok[b] is not None:
                c.t.wait_tok(c.bank_tok[b])
            h = None
            for k in range(DC):
                h = c.t.matmul(c.psum[b][:, 0:512],
                               h1[:, k * TOK + t * 128: k * TOK + t * 128 + 128],
                               w2_sl(k, n * 512, 512), start=(k == 0), stop=(k == DC - 1))
            mtk = h.inc(ms)
            c.v.wait_tok(mtk)
            h2 = c.v.tensor_tensor(stage[:, n * 512: n * 512 + 512],
                                   c.psum[b][:, 0:512], bbf[:, n * 512:(n + 1) * 512], ALU.add)
            tok = h2.inc(ev)
            c.bank_tok[b] = tok
        c.q.wait_tok(tok)
        c.dma("q", ktm_d[t * 128:(t + 1) * 128, :], stage[:, 0:1024])
        c.q.wait(c.dma_sems["q"], c.semctr[c.dma_sems["q"]])
        c.v.wait(c.dma_sems["q"], c.semctr[c.dma_sems["q"]])
    for m in range(DC):
        tok = None
        for n in range(NT):
            b = n % 4 + 4
            if c.bank_tok[b] is not None:
                c.t.wait_tok(c.bank_tok[b])
            h = None
            for k in range(DC):
                h = c.t.matmul(c.psum[b][:, 0:512], w2_sl(k, m * 128, 128),
                               h1[:, k * TOK + n * 512: k * TOK + n * 512 + 512],
                               start=(k == 0), stop=(k == DC - 1))
            mtk = h.inc(ms)
            c.v.wait_tok(mtk)
            h2 = c.v.tensor_scalar(stage[:, n * 512:(n + 1) * 512], c.psum[b][:, 0:512],
                                   b2k[:, m:m + 1], None, ALU.add)
            tok = h2.inc(ev)
            c.bank_tok[b] = tok
        c.q.wait_tok(tok)
        c.dma("q", kfm_d[m * 128:(m + 1) * 128, :], stage[:, 0:2048])
        c.q.wait(c.dma_sems["q"], c.semctr[c.dma_sems["q"]])
        c.v.wait(c.dma_sems["q"], c.semctr[c.dma_sems["q"]])
    c.barrier()
    c.fb("b2_w2x")
    c.ff("b2_b2k")
    mlp_h1("b2_W1q", "b2_b1qc")
    w2, w2_sl = load_weight_bf16(c, c.dram("b2_W2q", [D, D], F32, kind="ExternalInput"), "b2_w2x", D, D)
    b2q = load_f32(c, c.dram("b2_b2qc", [128, DC], F32, kind="ExternalInput"), "b2_b2q", 128, DC)
    c.barrier()
    c.fb("b2_xn")
    q_fm = c.ab("b2_q", DC * TOK)
    gemm(c, DC, DC, NT,
         lambda k, m: w2_sl(k, m * 128, 128),
         lambda k, n: h1[:, k * TOK + n * 512: k * TOK + n * 512 + 512],
         lambda m, n, ps, mtk: (
             c.v.wait_tok(mtk),
             c.v.tensor_scalar(q_fm[:, m * TOK + n * 512: m * TOK + n * 512 + 512],
                               ps[:, 0:512], b2q[:, m:m + 1], SCALE,
                               ALU.add, ALU.mult).inc(ev))[-1])
    c.barrier()
    c.fb("b2_w2x", "b2_h1", "b2_stage")
    c.ff("b2_b2q", "b2_brow", "b2_bb")


    if upto == 2:
        _b2_passthrough(c, x_in, x_out)
        return
    vt_sb = c.ab("b2_vt", 16 * 1024)
    kt_stage = c.ab("b2_kt", 1024)
    Sl = c.ab("b2_S", DC * 1024)
    for t in range(16):
        c.dma("p", vt_sb[:, t * 1024:(t + 1) * 1024], vtm_d[t * 128:(t + 1) * 128, :])
    c.barrier()
    dtok1 = None
    for mh in range(2):
        for t in range(16):
            if dtok1 is not None:
                c.q.wait_tok(dtok1)
            dt = c.dma("q", kt_stage[:, 0:1024], ktm_d[t * 128:(t + 1) * 128, :])
            c.t.wait_tok(dt)
            h = None
            for m in range(4):
                mm2 = mh * 4 + m
                for n in range(2):
                    b = m * 2 + n
                    h = c.t.matmul(c.psum[b][:, 0:512],
                                   kt_stage[:, mm2 * 128: mm2 * 128 + 128],
                                   vt_sb[:, t * 1024 + n * 512: t * 1024 + n * 512 + 512],
                                   start=(t == 0), stop=(t == 15))
            dtok1 = h.inc(ms)
        for m in range(4):
            mm2 = mh * 4 + m
            for n in range(2):
                b = m * 2 + n
                c.v.wait_tok(dtok1)
                h2 = c.v.tensor_copy(Sl[:, mm2 * 1024 + n * 512: mm2 * 1024 + n * 512 + 512],
                                     c.psum[b][:, 0:512])
                h2.inc(ev)
        c.barrier()
        dtok1 = None
    s_in = c.dram("b2_sin", [D, 1024], BF16)
    s_out = c.dram("b2_sout", [4 * D, 1024], BF16)
    for m in range(DC):
        c.dma("q", s_in[m * 128:(m + 1) * 128, :], Sl[:, m * 1024:(m + 1) * 1024])
    c.wait_all_dma()
    c.barrier()
    ctok = c.collective("AllGather", ALU.bypass, GROUPS_BATCH, s_in, s_out)
    c.p.wait_tok(ctok)
    rmask = load_f32(c, c.dram(rmask_name, [128, 4], F32, kind="ExternalInput"), "b2_rm", 128, 4)
    Sp_f = c.af("b2_spf", DC * 1024)
    sr_b = c.ab("b2_sr", 1024)
    c.v.memset(Sp_f[:, 0:DC * 1024], 0.0)
    c.barrier()
    vtok1 = None
    for r in range(4):
        for m in range(DC):
            if vtok1 is not None:
                c.p.wait_tok(vtok1)
            dt = c.dma("p", sr_b[:, 0:1024],
                       s_out[r * D + m * 128: r * D + m * 128 + 128, :])
            c.v.wait_tok(dt)
            h2 = c.v.scalar_tensor_tensor(Sp_f[:, m * 1024:(m + 1) * 1024],
                                          sr_b[:, 0:1024],
                                          rmask[:, r:r + 1],
                                          Sp_f[:, m * 1024:(m + 1) * 1024],
                                          ALU.mult, ALU.add)
            vtok1 = h2.inc(ev)
    c.barrier()
    c.fb("b2_sr")
    Sp = Sl
    h2 = None
    for m in range(DC):
        h2 = c.v.tensor_copy(Sp[:, m * 1024:(m + 1) * 1024], Sp_f[:, m * 1024:(m + 1) * 1024])
    h2.inc(ev)
    c.barrier()
    c.ff("b2_spf")


    if upto == 3:
        _b2_passthrough(c, x_in, x_out)
        return
    attn_raw = c.af("b2_attn", 16384)
    sc_sb = c.ab("b2_sc", 1024)
    masks = load_f32(c, c.dram("b2_masks", [128, 4 * 512], F32, kind="ExternalInput"), "b2_msk", 128, 4 * 512)
    c.barrier()
    kst_guard = [None, None]
    do_sp = upto != 47
    do_causal = upto != 46
    for n in range(NT):
        for mh in range(2):
            last_mm = [None] * 4
            for m in range(4):
                bmk = m
                if c.bank_tok[bmk] is not None:
                    c.t.wait_tok(c.bank_tok[bmk])
                mm2 = mh * 4 + m
                if do_sp:
                    for k in range(DC):
                        h = c.t.matmul(c.psum[bmk][:, 0:512],
                                       Sp[:, k * 1024 + mm2 * 128: k * 1024 + mm2 * 128 + 128],
                                       q_fm[:, k * TOK + n * 512: k * TOK + n * 512 + 512],
                                       start=(k == 0), stop=(not do_causal and k == DC - 1))
                        last_mm[m] = h
            ntk = (4 * n + 4) if do_causal else 0
            for tk in range(ntk):
                sb2 = 4 + tk % 2
                scsl = sc_sb[:, (tk % 2) * 512:(tk % 2) * 512 + 512]
                if c.bank_tok[sb2] is not None:
                    c.t.wait_tok(c.bank_tok[sb2])
                ksl = kt_stage[:, 0:1024]
                if kst_guard[0] is not None:
                    c.q.wait_tok(kst_guard[0])
                dt = c.dma("q", ksl.rearrange("p (k cc) -> p k cc", cc=128),
                           kfm_d[:, tk * 128:(tk + 1) * 128].rearrange(
                               "(k p) cc -> p k cc", p=128))
                c.t.wait_tok(dt)
                h = None
                for k in range(DC):
                    h = c.t.matmul(c.psum[sb2][:, 0:512],
                                   ksl[:, k * 128:(k + 1) * 128],
                                   q_fm[:, k * TOK + n * 512: k * TOK + n * 512 + 512],
                                   start=(k == 0), stop=(k == DC - 1))
                stk = h.inc(ms)
                kst_guard[0] = stk
                c.v.wait_tok(stk)
                if tk >= 4 * n:
                    j = tk - 4 * n
                    h2 = c.v.tensor_tensor(scsl, c.psum[sb2][:, 0:512],
                                           masks[:, j * 512:(j + 1) * 512], ALU.mult)
                else:
                    h2 = c.v.tensor_copy(scsl, c.psum[sb2][:, 0:512])
                sctk = h2.inc(ev)
                c.bank_tok[sb2] = sctk
                c.t.wait_tok(sctk)
                for m in range(4):
                    mm2 = mh * 4 + m
                    h = c.t.matmul(c.psum[m][:, 0:512],
                                   vt_sb[:, tk * 1024 + mm2 * 128: tk * 1024 + mm2 * 128 + 128],
                                   scsl, start=(not do_sp and tk == 0), stop=(tk == ntk - 1))
                    if tk == ntk - 1:
                        last_mm[m] = h
            for m in range(4):
                mm2 = mh * 4 + m
                mtk = last_mm[m].inc(ms)
                c.v.wait_tok(mtk)
                h2 = c.v.tensor_copy(
                    attn_raw[:, mm2 * TOK + n * 512: mm2 * TOK + n * 512 + 512],
                    c.psum[m][:, 0:512])
                c.bank_tok[m] = h2.inc(ev)
    c.barrier()
    if upto in (45, 46, 47):
        for j in range(DC):
            c.dma("q", x_out[j * 128:(j + 1) * 128, :], attn_raw[:, j * TOK:(j + 1) * TOK])
        c.barrier()
        return
    c.fb("b2_q", "b2_vt", "b2_kt", "b2_sc", "b2_S")
    c.ff("b2_rm", "b2_msk")

    if upto == 4:
        _b2_passthrough(c, x_in, x_out)
        return
    lnw = load_f32(c, c.dram("b2_lnwc", [128, DC], F32, kind="ExternalInput"), "b2_lnw", 128, DC)
    lnb = load_f32(c, c.dram("b2_lnbc", [128, DC], F32, kind="ExternalInput"), "b2_lnb", 128, DC)
    attn2 = c.dram("b2_attn2", [D, TOK], BF16)
    a2b = c.ab("b2_a2b", DC * TOK)
    c.barrier()
    ln2tok = emit_layernorm(c, "b2ln2", lambda j: attn_raw[:, j * TOK:(j + 1) * TOK],
                            lambda j: a2b[:, j * TOK:(j + 1) * TOK],
                            gamma_col=lambda j: lnw[:, j:j + 1],
                            beta_col=lambda j: lnb[:, j:j + 1])
    c.q.wait_tok(ln2tok)
    for j in range(DC):
        c.dma("q", attn2[j * 128:(j + 1) * 128, :], a2b[:, j * TOK:(j + 1) * TOK])
    c.barrier()
    c.fb("b2_a2b")
    if upto == 48:
        pt = c.af("b2_pt48", TOK)
        for j in range(DC):
            t = c.dma("p", pt[:], attn2[j * 128:(j + 1) * 128, :])
            c.q.wait_tok(t)
            c.dma("q", x_out[j * 128:(j + 1) * 128, :], pt[:])
            c.barrier()
        c.barrier()
        return
    c.ff("b2_attn", "b2_lnw", "b2_lnb")


    if upto == 5:
        _b2_passthrough(c, x_in, x_out)
        return
    xn2 = c.ab("b2_xn2", DC * TOK)
    for j in range(DC):
        c.dma("p", xn2[:, j * TOK:(j + 1) * TOK], xnd[j * 128:(j + 1) * 128, :])
    acc = c.af("b2_acc", 16384)
    gwf = c.af("b2_gw", 64)
    gwd = c.dram("b2_gWT", [D, 8], F32, kind="ExternalInput")
    for k in range(DC):
        c.dma("q", gwf[:, k * 8:(k + 1) * 8], gwd[k * 128:(k + 1) * 128, :])
    gbt = load_f32(c, c.dram("b2_gbias", [128, 8], F32, kind="ExternalInput"), "b2_gbt", 128, 8)
    xnf_t = c.af("b2_xnf32", 2048)
    rtf = c.af("b2_rt", 128 * 6 + 64 + 16 * 3 + 128)
    lg = rtf[:, 0:128]
    msk2 = rtf[:, 128:256]
    scr2 = rtf[:, 256:384]
    dbuf2 = rtf[:, 384:512]
    ebuf = rtf[:, 512:640]
    tpb_s = rtf[:, 640:768]
    wbuf2 = rtf[:, 768:832]
    mseg2 = rtf[:, 832:848]
    ssum = rtf[:, 848:864]
    tpcol = rtf[:, 864:880]
    neg30b = rtf[:, 880:1008]
    c.v.memset(neg30b, -1e30)
    c.v.memset(acc[:, 0:16384], 0.0)
    c.barrier()
    xnftok = None
    for t in range(16):
        b = t % 4
        if c.bank_tok[b] is not None:
            c.t.wait_tok(c.bank_tok[b])
        if xnftok is not None:
            c.q.wait_tok(xnftok)
        dtx = c.dma("q", xnf_t[:, 0:1024].rearrange("p (k cc) -> p k cc", cc=128),
                    xnfd[:, t * 128:(t + 1) * 128].rearrange("(k p) cc -> p k cc", p=128))
        c.t.wait_tok(dtx)
        h = None
        for k in range(DC):
            h = c.t.matmul(c.psum[b][:, 0:8],
                           xnf_t[:, k * 128:(k + 1) * 128],
                           gwf[:, k * 8:(k + 1) * 8], start=(k == 0), stop=(k == DC - 1))
        mtk = h.inc(ms)
        xnftok = mtk
        c.v.wait_tok(mtk)
        h2 = c.v.tensor_tensor(lg[:, t * 8:(t + 1) * 8], c.psum[b][:, 0:8],
                               gbt[:, 0:8], ALU.add)
        c.bank_tok[b] = h2.inc(ev)
    c.barrier()
    c.ff("b2_xnf32")
    batched_topk(c, lg, msk2, scr2, dbuf2, wbuf2, mseg2, 16, 8, 1, neg30b)
    x3 = lambda tns: tns.rearrange("p (cc e) -> p cc e", e=8)
    m3 = mseg2.rearrange("p (cc e) -> p cc e", e=1)
    a1, a2v = bass.broadcast_tensor_aps(x3(lg), m3)
    h2 = c.v.tensor_tensor(x3(ebuf), a1, a2v, ALU.subtract)
    vt = h2.inc(ev)
    c.s.wait_tok(vt)
    h2 = c.s.activation(ebuf, ebuf, AF.Exp)
    st = h2.inc(es)
    c.v.wait_tok(st)
    e3 = x3(ebuf)
    w3 = wbuf2.rearrange("p (cc e) -> p cc e", e=4)
    c.v.tensor_tensor(w3[:, :, 0:4], e3[:, :, 0:4], e3[:, :, 4:8], ALU.add)
    c.v.tensor_tensor(w3[:, :, 0:2], w3[:, :, 0:2], w3[:, :, 2:4], ALU.add)
    h2 = c.v.tensor_tensor(ssum.rearrange("p (cc e) -> p cc e", e=1),
                           w3[:, :, 0:1], w3[:, :, 1:2], ALU.add)
    vt = h2.inc(ev)
    c.s.wait_tok(vt)
    h2 = c.s.activation(ssum, ssum, AF.Copy)
    st = h2.inc(es)
    c.v.wait_tok(st)
    h2 = c.v.reciprocal(tpcol, ssum)
    vt = h2.inc(ev)
    c.s.wait_tok(vt)
    h2 = c.s.activation(tpcol, tpcol, AF.Copy)
    st = h2.inc(es)
    c.v.wait_tok(st)
    t3v = tpcol.rearrange("p (cc e) -> p cc e", e=1)
    a1, a2v = bass.broadcast_tensor_aps(x3(msk2), t3v)
    h2 = c.v.tensor_tensor(x3(scr2), a1, a2v, ALU.mult)
    h2.inc(ev)
    c.barrier()
    ohfm = c.af("b2_ohfm", TOK)
    cwpre = c.af("b2_cwpre", TOK)
    for t in range(16):
        b = t % 4
        if c.bank_tok[b] is not None:
            c.t.wait_tok(c.bank_tok[b])
        h = c.t.transpose(c.psum[b][0:8, 0:128], msk2[:, t * 8:(t + 1) * 8], c.ident_f[:])
        mtk = h.inc(ms)
        c.v.wait_tok(mtk)
        h2 = c.v.tensor_copy(ohfm[0:8, t * 128:(t + 1) * 128], c.psum[b][0:8, 0:128])
        c.bank_tok[b] = h2.inc(ev)
        b2 = 4 + t % 4
        if c.bank_tok[b2] is not None:
            c.t.wait_tok(c.bank_tok[b2])
        h = c.t.transpose(c.psum[b2][0:8, 0:128], scr2[:, t * 8:(t + 1) * 8], c.ident_f[:])
        mtk = h.inc(ms)
        c.v.wait_tok(mtk)
        h2 = c.v.tensor_copy(cwpre[0:8, t * 128:(t + 1) * 128], c.psum[b2][0:8, 0:128])
        c.bank_tok[b2] = h2.inc(ev)
    c.barrier()
    cum = c.af("b2_cum", TOK)
    c.barrier()
    h2 = c.v.tensor_tensor_scan(cum[0:8, :], ohfm[0:8, :], ohfm[0:8, :], 0.0,
                                ALU.add, ALU.bypass)
    vt = h2.inc(ev)
    cnt_in = c.dram("b2_cntin", [8, 1], F32)
    cnt_out = c.dram("b2_cntout", [64, 1], F32)
    c.q.wait_tok(vt)
    c.dma("q", cnt_in, cum[0:8, TOK - 1:TOK])
    c.wait_all_dma()
    c.barrier()
    ctok = c.collective("AllGather", ALU.bypass, GROUP_ALL, cnt_in, cnt_out)
    c.p.wait_tok(ctok)
    cnt_sb = c.af("b2_cnt", 8 + 8)
    offc = cnt_sb[:, 8:9]
    rmask8 = load_f32(c, c.dram("b2_rmask8", [8, 1], F32, kind="ExternalInput"), "b2_rm8", 8, 1)
    c.dma("p", cnt_sb[0:8, 0:8], cnt_out.rearrange("(r e) one -> r (e one)", e=8))
    c.barrier()
    h = c.t.matmul(c.psum[6][0:8, 0:1], cnt_sb[0:8, 0:8], rmask8[0:8, 0:1],
                   start=True, stop=True)
    mtk = h.inc(ms)
    c.s.wait_tok(mtk)
    h2 = c.s.activation(offc[0:8, 0:1], c.psum[6][0:8, 0:1], AF.Copy)
    st = h2.inc(es)
    c.v.wait_tok(st)
    c.v.tensor_scalar(cum[0:8, :], cum[0:8, :], offc[0:8, 0:1], None, ALU.add)
    c.v.tensor_scalar(cum[0:8, :], cum[0:8, :], 2560.0, None, ALU.is_le)
    c.v.tensor_tensor(cum[0:8, :], cum[0:8, :], ohfm[0:8, :], ALU.mult)
    h2 = c.v.tensor_tensor(cwpre[0:8, :], cwpre[0:8, :], cum[0:8, :], ALU.mult)
    vt = h2.inc(ev)
    cwdram = c.dram("b2_cwdram", [8, TOK], F32)
    c.q.wait_tok(vt)
    c.dma("q", cwdram, cwpre[0:8, :])
    cwfmb = c.ab("b2_cwfmb", TOK)
    h2 = c.v.tensor_copy(cwfmb[0:8, :], cwpre[0:8, :])
    h2.inc(ev)
    b2w, b2w_sl = load_weight_bf16(c, c.dram("b2_eb2", [8, D], F32, kind="ExternalInput"), "b2_eb2s", 8, D)
    c.barrier()

    def ev_add_acc2(m, n, ps, mtk):
        c.v.wait_tok(mtk)
        h2 = c.v.tensor_tensor(acc[:, m * TOK + n * 512: m * TOK + n * 512 + 512],
                               acc[:, m * TOK + n * 512: m * TOK + n * 512 + 512],
                               ps[:, 0:512], ALU.add)
        return h2.inc(ev)

    gemm(c, 1, DC, NT,
         lambda k, m: b2w[0:8, m * 128:(m + 1) * 128],
         lambda k, n: cwfmb[0:8, n * 512:(n + 1) * 512],
         ev_add_acc2)
    c.barrier()
    if upto == 59:
        c.dma("q", x_out[0:8, :], cwpre[0:8, :])
        c.dma("q", x_out[8:16, :], cum[0:8, :])
        c.dma("q", x_out[16:24, :], ohfm[0:8, :])
        c.dma("q", x_out[128:256, 0:128], lg)
        c.dma("q", x_out[256:384, 0:128], msk2)
        c.dma("q", x_out[384:512, 0:16], tpcol)
        c.dma("q", x_out[512:520, 0:9], cnt_sb[0:8, 0:9])
        c.dma("p", x_out[520:648, 256:512], xn2[:, 0:256])
        c.dma("p", x_out[648:776, 256:512], xn2[:, TOK:TOK + 256])
        c.barrier()
        return
    c.fb("b2_cwfmb", "b2_eb2s")
    c.ff("b2_rt", "b2_ohfm", "b2_cum", "b2_cnt", "b2_rm8", "b2_cwpre")

    if upto == 6:
        _b2_passthrough(c, x_in, x_out)
        return
    w1d = c.dram("b2_ew1", [8 * 32 * DC * 128, 128], F32, kind="ExternalInput")
    w2d = c.dram("b2_ew2", [8 * 4096, D], F32, kind="ExternalInput")
    b1d = c.dram("b2_eb1", [8 * 128, 32], F32, kind="ExternalInput")
    he = c.ab("b2_he", 32 * 512)
    w1s = c.ab("b2_w1s", 2 * 1024)
    w2s = c.ab("b2_w2s", 2 * 1024)
    cwbf = c.af("b2_cwb", TOK + TOK + 32 + 512)
    cwrow = cwbf[0:1, 0:TOK]
    cwb = cwbf[:, TOK:2 * TOK]
    b1e = cwbf[:, 2 * TOK:2 * TOK + 32]
    tmp5 = cwbf[:, 2 * TOK + 32:2 * TOK + 32 + 512]
    exlist = [0] if upto == 69 else list(range(8))
    if upto == 69:
        c.v.memset(acc[:, 0:16384], 0.0)
    for e in exlist:
        c.dma("q", b1e, b1d[e * 128:(e + 1) * 128, :])
        c.dma("q", cwrow, cwdram[e:e + 1, :])
        c.barrier()
        if upto == 69:
            c.v.memset(cwb[:, 0:TOK], 1.0)
        else:
            bcast_row_to(c, lambda n: cwrow[0:1, n * 512:(n + 1) * 512],
                         lambda n: cwb[:, n * 512:(n + 1) * 512], NT, banks=(6, 7),
                         evac_eng="v")
        c.barrier()
        for n in range(NT):
            wtoks = [None, None]
            for hc in range(32):
                sb = hc % 2
                if wtoks[sb] is not None:
                    c.p.wait_tok(wtoks[sb])
                base = (e * 32 + hc) * DC * 128
                for kk2 in range(DC):
                    dt = c.dma("p", w1s[:, sb * 1024 + kk2 * 128: sb * 1024 + (kk2 + 1) * 128],
                               w1d[base + kk2 * 128: base + (kk2 + 1) * 128, :])
                c.t.wait_tok(dt)
                b = 4 + hc % 2
                if c.bank_tok[b] is not None:
                    c.t.wait_tok(c.bank_tok[b])
                h = None
                for k in range(DC):
                    h = c.t.matmul(c.psum[b][:, 0:512],
                                   w1s[:, sb * 1024 + k * 128: sb * 1024 + k * 128 + 128],
                                   xn2[:, k * TOK + n * 512: k * TOK + n * 512 + 512],
                                   start=(k == 0), stop=(k == DC - 1))
                mtk = h.inc(ms)
                wtoks[sb] = mtk
                c.s.wait_tok(mtk)
                h2 = c.s.activation(he[:, hc * 512:(hc + 1) * 512], c.psum[b][:, 0:512],
                                    AF.Relu, bias=b1e[:, hc:hc + 1])
                c.bank_tok[b] = h2.inc(es)
            c.barrier()
            if upto == 70 and n == 0:
                c.dma("p", x_out[0:128, :], he[:, 0:TOK])
                c.dma("p", x_out[128:256, :], he[:, TOK:2 * TOK])
                c.barrier()
                return
            wtoks = [None, None]
            for hc in range(32):
                sb = hc % 2
                if wtoks[sb] is not None:
                    c.p.wait_tok(wtoks[sb])
                dt = c.dma("p", w2s[:, sb * 1024:(sb + 1) * 1024],
                           w2d[e * 4096 + hc * 128: e * 4096 + hc * 128 + 128, :])
                c.t.wait_tok(dt)
                h = None
                for m in range(DC):
                    h = c.t.matmul(c.psum[m][:, 0:512],
                                   w2s[:, sb * 1024 + m * 128: sb * 1024 + m * 128 + 128],
                                   he[:, hc * 512:(hc + 1) * 512],
                                   start=(hc == 0), stop=(hc == 31))
                wtoks[sb] = h.inc(ms)
            c.t.wait_tok(wtoks[0])
            c.t.wait_tok(wtoks[1])
            for m in range(DC):
                h = c.t.nop()
                mtk = h.inc(ms)
                c.v.wait_tok(mtk)
                c.v.tensor_tensor(tmp5, c.psum[m][:, 0:512], cwb[:, n * 512:(n + 1) * 512],
                                  ALU.mult)
                h2 = c.v.tensor_tensor(acc[:, m * TOK + n * 512: m * TOK + n * 512 + 512],
                                       acc[:, m * TOK + n * 512: m * TOK + n * 512 + 512],
                                       tmp5, ALU.add)
                c.bank_tok[m] = h2.inc(ev)
            c.barrier()
    c.fb("b2_he", "b2_w1s", "b2_w2s", "b2_xn2")
    c.ff("b2_gw")
    if upto in (68, 69):
        for j in range(DC):
            c.dma("q", x_out[j * 128:(j + 1) * 128, :], acc[:, j * TOK:(j + 1) * TOK])
        c.barrier()
        return
    c.ff("b2_cwb", "b2_gbt")


    if upto == 7:
        _b2_passthrough(c, x_in, x_out)
        return
    xj = c.af("b2_xj", TOK)
    af2 = c.af("b2_af", TOK)
    for j in range(DC):
        t1tok = c.dma("q", xj[:], x_in[j * 128:(j + 1) * 128, :])
        t2tok = c.dma("p", af2[:], attn2[j * 128:(j + 1) * 128, :])
        c.v.wait_tok(t1tok)
        c.v.wait_tok(t2tok)
        c.v.tensor_tensor(xj[:], xj[:], acc[:, j * TOK:(j + 1) * TOK], ALU.add)
        h = c.v.tensor_tensor(xj[:], xj[:], af2[:], ALU.add)
        vt = h.inc(ev)
        c.q.wait_tok(vt)
        c.dma("q", x_out[j * 128:(j + 1) * 128, :], xj[:])
        c.barrier()
    c.ff("b2_xj", "b2_af", "b2_acc")
    c.barrier()


def prep_block2(inputs, core):
    A = np.ascontiguousarray
    d = {}
    for nm in ("q", "k", "v"):
        d[f"b2_W1{nm}"] = A(inputs[f"L2_attn_W1{nm}"])
        d[f"b2_W2{nm}"] = A(inputs[f"L2_attn_W2{nm}"])
        d[f"b2_b1{nm}c"] = A(inputs[f"L2_attn_b1{nm}"].reshape(DC, 128).T)
    d["b2_b2qc"] = A(inputs["L2_attn_b2q"].reshape(DC, 128).T)
    d["b2_b2kc"] = A(inputs["L2_attn_b2k"].reshape(DC, 128).T)
    d["b2_b2kr"] = A(inputs["L2_attn_b2k"].reshape(1, D))
    d["b2_b2vr"] = A(inputs["L2_attn_b2v"].reshape(1, D))
    d["b2_lnwc"] = A(inputs["L2_attn_ln_w"].reshape(DC, 128).T)
    d["b2_lnbc"] = A(inputs["L2_attn_ln_b"].reshape(DC, 128).T)
    p = core % 4
    rm = np.zeros((128, 4), np.float32)
    rm[:, :p] = 1.0
    d["b2_rmask"] = rm
    rm8 = np.zeros((8, 1), np.float32)
    rm8[:core] = 1.0
    d["b2_rmask8"] = rm8
    mk = np.zeros((128, 4 * 512), np.float32)
    for j in range(4):
        r = np.arange(128)[:, None]
        f = np.arange(512)[None, :]
        mk[:, j * 512:(j + 1) * 512] = (j * 128 + r <= f).astype(np.float32)
    d["b2_masks"] = mk
    d["b2_gWT"] = A(inputs["L2_ffn_gate_w"].T)
    d["b2_gbias"] = A(np.broadcast_to(inputs["L2_ffn_gate_b"], (128, 8)))
    w1 = inputs["L2_ffn_w1"].reshape(8, DC, 128, 32, 128)
    d["b2_ew1"] = A(w1.transpose(0, 3, 1, 2, 4).reshape(8 * 32 * DC * 128, 128))
    d["b2_ew2"] = A(inputs["L2_ffn_w2"].reshape(8 * 4096, D))
    d["b2_eb1"] = A(inputs["L2_ffn_b1"].reshape(8, 32, 128).transpose(0, 2, 1).reshape(8 * 128, 32))
    d["b2_eb2"] = A(inputs["L2_ffn_b2"])
    return d


B3S = 256
B3NCH = SEQ // B3S
B3CPAD = SEQ + 8


def emit_block3(c, x_in, x_out, upto=99):
    ev = c.sem("ev_v")
    es = c.sem("ev_s")
    ms = c.sem("mm")
    NST = 16

    xn = c.ab("b3_xn", DC * TOK)
    x_f = c.af("b3_x", 16384)
    for j in range(DC):
        c.dma("q", x_f[:, j * TOK:(j + 1) * TOK], x_in[j * 128:(j + 1) * 128, :])
    c.barrier()
    lntok = emit_layernorm(c, "b3ln", lambda j: x_f[:, j * TOK:(j + 1) * TOK],
                           lambda j: xn[:, j * TOK:(j + 1) * TOK])
    c.ff("b3_x")
    xnd = c.dram("b3_xnd", [D, TOK], BF16)
    c.q.wait_tok(lntok)
    for j in range(DC):
        c.dma("q", xnd[j * 128:(j + 1) * 128, :], xn[:, j * TOK:(j + 1) * TOK])
    c.wait_all_dma()
    c.barrier()
    c.fb("b3_xn")
    xng = c.dram("b3_xng", [4 * D, TOK], BF16)
    ctok = c.collective("AllGather", ALU.bypass, GROUPS_BATCH, xnd, xng)
    c.p.wait_tok(ctok)
    c.barrier()

    wind = c.dram("b3_WinT", [D, 1024], F32, kind="ExternalInput")
    xa = c.ab("b3_xa", 4 * B3CPAD)
    xs_buf = c.ab("b3_xs", 4096)
    zd = c.dram("b3_z", [512, SEQ], BF16)
    zstage = c.ab("b3_zst", 2048)
    for half in range(2):
        win = c.ab("b3_win", 4096)
        for k in range(DC):
            c.dma("p", win[:, k * 512:(k + 1) * 512],
                  wind[k * 128:(k + 1) * 128, half * 512:(half + 1) * 512])
        c.barrier()
        xstok = None
        for n in range(16):
            r, lc = n // 4, (n % 4) * 512
            if xstok is not None:
                c.q.wait_tok(xstok)
            dt = None
            for k in range(DC):
                dt = c.dma("q", xs_buf[:, k * 512:(k + 1) * 512],
                           xng[r * D + k * 128: r * D + k * 128 + 128, lc:lc + 512])
            c.t.wait_tok(dt)
            h = None
            zt = None
            for m in range(4):
                b = m % 4
                if c.bank_tok[b] is not None:
                    c.t.wait_tok(c.bank_tok[b])
                for k in range(DC):
                    h = c.t.matmul(c.psum[b][:, 0:512],
                                   win[:, k * 512 + m * 128: k * 512 + m * 128 + 128],
                                   xs_buf[:, k * 512:(k + 1) * 512],
                                   start=(k == 0), stop=(k == DC - 1))
                mtk = h.inc(ms)
                xstok = mtk
                c.v.wait_tok(mtk)
                if half == 0:
                    h2 = c.v.tensor_copy(
                        xa[:, m * B3CPAD + 3 + n * 512: m * B3CPAD + 3 + n * 512 + 512],
                        c.psum[b][:, 0:512])
                    c.bank_tok[b] = h2.inc(ev)
                else:
                    h2 = c.v.tensor_copy(zstage[:, m * 512:(m + 1) * 512], c.psum[b][:, 0:512])
                    zt = h2.inc(ev)
                    c.bank_tok[b] = zt
            if half == 1:
                c.q.wait_tok(zt)
                for mz in range(4):
                    c.dma("q", zd[mz * 128:(mz + 1) * 128, n * 512:(n + 1) * 512],
                          zstage[:, mz * 512:(mz + 1) * 512])
                c.v.wait(c.dma_sems["q"], c.semctr[c.dma_sems["q"]])
        c.barrier()
        c.fb("b3_win")
    c.fb("b3_xs", "b3_zst")
    for cc in range(4):
        c.v.memset(xa[:, cc * B3CPAD:cc * B3CPAD + 3], 0.0)
    cwt = load_f32(c, c.dram("b3_convw", [128, 16], F32, kind="ExternalInput"), "b3_cwt", 128, 16)
    cbt = load_f32(c, c.dram("b3_convb", [128, 4], F32, kind="ExternalInput"), "b3_cbt", 128, 4)
    cacc = c.af("b3_cacc", SEQ)
    c.barrier()
    stok_prev = None
    for cc in range(4):
        base = cc * B3CPAD
        if stok_prev is not None:
            c.v.wait_tok(stok_prev)
        c.v.tensor_scalar(cacc[:, 0:SEQ], xa[:, base + 0: base + SEQ],
                          cwt[:, cc * 4 + 0: cc * 4 + 1], None, ALU.mult)
        for k in range(1, 4):
            h2 = c.v.scalar_tensor_tensor(cacc[:, 0:SEQ], xa[:, base + k: base + k + SEQ],
                                          cwt[:, cc * 4 + k: cc * 4 + k + 1],
                                          cacc[:, 0:SEQ], ALU.mult, ALU.add)
        vt = h2.inc(ev)
        c.s.wait_tok(vt)
        h2 = c.s.activation(xa[:, base + 3: base + 3 + SEQ], cacc[:, 0:SEQ],
                            AF.Silu, bias=cbt[:, cc:cc + 1])
        stok_prev = h2.inc(es)
    c.barrier()
    c.ff("b3_cacc")
    if upto == 21:
        pt = c.af("b3_pt21", TOK)
        for cc in range(4):
            for half in range(4):
                sl = xa[:, cc * B3CPAD + 3 + half * TOK: cc * B3CPAD + 3 + half * TOK + TOK]
                h2 = c.v.tensor_copy(pt[:], sl)
                vt = h2.inc(c.sem("ev_v"))
                c.q.wait_tok(vt)
                c.dma("q", x_out[(cc * 4 + half) * 64:(cc * 4 + half) * 64 + 64, 0:TOK],
                      pt[0:64, :])
                c.barrier()
        return
    xpw, xpw_sl = load_weight_bf16(c, c.dram("b3_xprojT", [512, 96], F32, kind="ExternalInput"), "b3_xpw", 512, 96)
    xp_part = c.dram("b3_xppart", [96, SEQ], F32)
    xp_red = c.dram("b3_xpred", [96, SEQ], F32)
    xstage = c.af("b3_xstage", 1024)
    c.barrier()
    for n in range(16):
        b = n % 2
        if c.bank_tok[b] is not None:
            c.t.wait_tok(c.bank_tok[b])
        h = None
        for cc in range(4):
            h = c.t.matmul(c.psum[b][0:96, 0:512], xpw_sl(cc, 0, 96),
                           xa[:, cc * B3CPAD + 3 + n * 512: cc * B3CPAD + 3 + n * 512 + 512],
                           start=(cc == 0), stop=(cc == 3))
        mtk = h.inc(ms)
        c.v.wait_tok(mtk)
        h2 = c.v.tensor_copy(xstage[0:96, (n % 2) * 512:(n % 2) * 512 + 512], c.psum[b][0:96, 0:512])
        vt = h2.inc(ev)
        c.bank_tok[b] = vt
        c.q.wait_tok(vt)
        c.dma("q", xp_part[:, n * 512:(n + 1) * 512], xstage[0:96, (n % 2) * 512:(n % 2) * 512 + 512])
        c.v.wait(c.dma_sems["q"], c.semctr[c.dma_sems["q"]])
    c.wait_all_dma()
    c.barrier()
    c.ff("b3_xstage")
    c.fb("b3_xpw")
    ctok = c.collective("AllReduce", ALU.add, GROUPS_BATCH, xp_part, xp_red)
    c.p.wait_tok(ctok)
    c.barrier()
    if upto == 23:
        pt = c.af("b3_pt23", TOK)
        for half in range(4):
            t = c.dma("q", pt[0:96, :], xp_part[:, half * TOK:(half + 1) * TOK])
            c.q.wait_tok(t)
            c.dma("q", x_out[half * 96: half * 96 + 96, :], pt[0:96, :])
            c.barrier()
        return
    if upto == 22:
        pt = c.af("b3_pt22", TOK)
        for half in range(4):
            t = c.dma("q", pt[0:96, :], xp_red[:, half * TOK:(half + 1) * TOK])
            c.q.wait_tok(t)
            c.dma("q", x_out[half * 96: half * 96 + 96, :], pt[0:96, :])
            c.barrier()
        return
    dlb = c.ab("b3_dlb", B3S)
    bcrow = c.ab("b3_bcrow", B3S)
    dtw, dtw_sl = load_weight_bf16(c, c.dram("b3_dtwT", [64, 512], F32, kind="ExternalInput"), "b3_dtw", 64, 512)
    dtb = load_f32(c, c.dram("b3_dtb", [128, 4], F32, kind="ExternalInput"), "b3_dtbc", 128, 4)
    alog = load_f32(c, c.dram("b3_alog", [128, 64], F32, kind="ExternalInput"), "b3_alogc", 128, 64)
    dpar = load_f32(c, c.dram("b3_dpar", [128, 4], F32, kind="ExternalInput"), "b3_dparc", 128, 4)
    selm = c.ab("b3_selm", 32 * 128)
    c.dma("q", selm[0:32, :], c.dram("b3_sel", [32, 32 * 128], BF16, kind="ExternalInput"))
    woutT, wout_sl = load_weight_bf16(c, c.dram("b3_WoutT", [512, 1024], F32, kind="ExternalInput"), "b3_wout", 512, 1024)
    negA = c.af("b3_negA", 64)
    c.barrier()
    h2 = c.s.activation(negA[:, 0:64], alog[:, 0:64], AF.Exp)
    st = h2.inc(es)
    c.v.wait_tok(st)
    h2 = c.v.tensor_scalar(negA[:, 0:64], negA[:, 0:64], -1.0, None, ALU.mult)
    natok = h2.inc(ev)
    c.s.wait_tok(natok)
    c.s.activation(negA[:, 0:64], negA[:, 0:64], AF.Copy)
    c.barrier()
    S = B3S
    dA = c.af("b3_dA", NST * S)
    dBu = c.af("b3_dBu", NST * S)
    hbuf = c.af("b3_h", NST * S)
    Bball = c.af("b3_Bb", NST * S)
    Cball = c.af("b3_Cb", NST * S)
    ycc = c.af("b3_y", S)
    wcc = c.af("b3_w", S)
    delta = c.af("b3_delta", S)
    hcols = c.af("b3_hcols", 64)
    zsil = c.af("b3_zs", S)
    ygb = c.ab("b3_ygb", 4 * S)
    ostage = c.ab("b3_ost", DC * S)
    outpart = c.dram("b3_outpart", [4 * D, TOK], BF16)
    outred = c.dram("b3_outred", [D, TOK], BF16)
    c.v.memset(hcols[:, 0:64], 0.0)
    c.barrier()
    ytk = None
    for ch in range(B3NCH):
        t0 = ch * S
        dt1 = c.dma("p", dlb[0:64, 0:S], xp_red[0:64, t0:t0 + S])
        dt2 = c.dma("p", bcrow[0:32, 0:S], xp_red[64:96, t0:t0 + S])
        c.t.wait_tok(dt1)
        c.t.wait_tok(dt2)
        for n in range(2 * NST):
            b = 4 + n % 2
            if c.bank_tok[b] is not None:
                c.t.wait_tok(c.bank_tok[b])
            h = c.t.matmul(c.psum[b][:, 0:S], selm[0:32, n * 128:(n + 1) * 128],
                           bcrow[0:32, 0:S], start=True, stop=True)
            mtk = h.inc(ms)
            dst = Bball if n < NST else Cball
            nn = n % NST
            c.s.wait_tok(mtk)
            h2 = c.s.activation(dst[:, nn * S:(nn + 1) * S], c.psum[b][:, 0:S], AF.Copy)
            c.bank_tok[b] = h2.inc(es)
        for cc in range(4):
            xasl = xa[:, cc * B3CPAD + 3 + t0: cc * B3CPAD + 3 + t0 + S]
            b = cc % 2
            if c.bank_tok[b] is not None:
                c.t.wait_tok(c.bank_tok[b])
            h = c.t.matmul(c.psum[b][:, 0:S], dtw[0:64, cc * 128:(cc + 1) * 128],
                           dlb[0:64, 0:S], start=True, stop=True)
            mtk = h.inc(ms)
            c.s.wait_tok(mtk)
            c.s.activation(delta[:, 0:S], c.psum[b][:, 0:S], AF.Abs, bias=dtb[:, cc:cc + 1])
            c.s.activation(delta[:, 0:S], delta[:, 0:S], AF.Exp, scale=c.cc(-1.0))
            h2 = c.s.activation(delta[:, 0:S], delta[:, 0:S], AF.Ln, bias=c.cc(1.0))
            st = h2.inc(es)
            c.v.wait_tok(st)
            c.v.scalar_tensor_tensor(wcc[:, 0:S], c.psum[b][:, 0:S], dtb[:, cc:cc + 1],
                                     delta[:, 0:S], ALU.add, ALU.bypass)
            h2 = c.v.scalar_tensor_tensor(wcc[:, 0:S], wcc[:, 0:S], 0.0,
                                          delta[:, 0:S], ALU.max, ALU.bypass)
            c.bank_tok[b] = h2.inc(ev)
            h2 = c.v.tensor_tensor(delta[:, 0:S], delta[:, 0:S], wcc[:, 0:S], ALU.add)
            vt = h2.inc(ev)
            c.s.wait_tok(vt)
            hs = None
            for nn in range(NST):
                hs = c.s.activation(dA[:, nn * S:(nn + 1) * S], delta[:, 0:S], AF.Exp,
                                    scale=negA[:, cc * 16 + nn: cc * 16 + nn + 1])
            st = hs.inc(es)
            c.v.wait_tok(st)
            c.v.tensor_tensor(wcc[:, 0:S], delta[:, 0:S], xasl, ALU.mult)
            h2 = None
            for nn in range(NST):
                h2 = c.v.tensor_tensor(dBu[:, nn * S:(nn + 1) * S], wcc[:, 0:S],
                                       Bball[:, nn * S:(nn + 1) * S], ALU.mult)
            first = True
            for nn in range(NST):
                c.v.tensor_tensor_scan(hbuf[:, nn * S:(nn + 1) * S],
                                       dA[:, nn * S:(nn + 1) * S],
                                       dBu[:, nn * S:(nn + 1) * S],
                                       hcols[:, cc * 16 + nn: cc * 16 + nn + 1],
                                       ALU.mult, ALU.add)
                if first:
                    c.v.tensor_tensor(ycc[:, 0:S], hbuf[:, nn * S:(nn + 1) * S],
                                      Cball[:, nn * S:(nn + 1) * S], ALU.mult)
                    first = False
                else:
                    c.v.tensor_tensor(wcc[:, 0:S], hbuf[:, nn * S:(nn + 1) * S],
                                      Cball[:, nn * S:(nn + 1) * S], ALU.mult)
                    c.v.tensor_tensor(ycc[:, 0:S], ycc[:, 0:S], wcc[:, 0:S], ALU.add)
            for nn in range(NST):
                c.v.tensor_copy(hcols[:, cc * 16 + nn: cc * 16 + nn + 1],
                                hbuf[:, nn * S + S - 1: nn * S + S])
            h2 = c.v.scalar_tensor_tensor(ycc[:, 0:S], xasl, dpar[:, cc:cc + 1],
                                          ycc[:, 0:S], ALU.mult, ALU.add)
            vt = h2.inc(ev)
            if ch + cc > 0:
                c.p.wait_tok(ytk)
            zt = c.dma("p", zsil[:, 0:S], zd[cc * 128:(cc + 1) * 128, t0:t0 + S])
            c.s.wait_tok(zt)
            h2 = c.s.activation(zsil[:, 0:S], zsil[:, 0:S], AF.Silu)
            st = h2.inc(es)
            c.v.wait_tok(st)
            h2 = c.v.tensor_tensor(ygb[:, cc * S:(cc + 1) * S], ycc[:, 0:S], zsil[:, 0:S], ALU.mult)
            ytk = h2.inc(ev)
            if upto == 26 and ch == 1 and cc == 0:
                c.q.wait_tok(ytk)
                c.dma("q", x_out[0:128, 0:S], dA[:, 0:S])
                c.dma("q", x_out[128:256, 0:S], dBu[:, 0:S])
                c.dma("q", x_out[256:384, 0:S], hbuf[:, 0:S])
                c.dma("q", x_out[384:512, 0:S], ycc[:, 0:S])
                c.dma("q", x_out[512:640, 0:S], delta[:, 0:S])
                c.dma("q", x_out[640:768, 0:64], hcols[:, 0:64])
                c.barrier()
                return
            if upto == 25 and ch == 0 and cc == 0:
                c.q.wait_tok(ytk)
                c.dma("q", x_out[0:128, 0:S], dA[:, 0:S])
                c.dma("q", x_out[128:256, 0:S], dBu[:, 0:S])
                c.dma("q", x_out[256:384, 0:S], hbuf[:, 0:S])
                c.dma("q", x_out[384:512, 0:S], ycc[:, 0:S])
                c.dma("q", x_out[512:640, 0:S], delta[:, 0:S])
                c.dma("q", x_out[640:768, 0:S], zsil[:, 0:S])
                c.dma("p", x_out[768:896, 0:S], ygb[:, 0:S])
                c.dma("p", x_out[896:1024, 0:S], Bball[:, 0:S])
                c.barrier()
                return
        c.t.wait_tok(ytk)
        lasth = [None] * DC
        ncc_out = 1 if upto == 28 else 4
        for dm in range(DC):
            if c.bank_tok[dm] is not None:
                c.t.wait_tok(c.bank_tok[dm])
            for cc2 in range(ncc_out):
                lasth[dm] = c.t.matmul(c.psum[dm][:, 0:S], wout_sl(cc2, dm * 128, 128),
                                       ygb[:, cc2 * S:(cc2 + 1) * S],
                                       start=(cc2 == 0), stop=(cc2 == ncc_out - 1))
        dmt = None
        for dm in range(DC):
            mtk = lasth[dm].inc(ms)
            c.v.wait_tok(mtk)
            h2 = c.v.tensor_copy(ostage[:, dm * S:(dm + 1) * S], c.psum[dm][:, 0:S])
            c.bank_tok[dm] = h2.inc(ev)
            dmt = c.bank_tok[dm]
        c.q.wait_tok(dmt)
        if upto == 29 and ch == 0:
            c.p.wait_tok(dmt)
            c.dma("p", x_out[0:128, 0:4 * S], ygb[:, 0:4 * S])
            c.dma("p", x_out[128:256, 0:8 * S], ostage[:, 0:8 * S])
            c.dma("q", x_out[256:384, 0:S], ycc[:, 0:S])
            c.dma("q", x_out[384:512, 0:S], zsil[:, 0:S])
            c.barrier()
            return
        g = t0 // TOK
        col0 = t0 % TOK
        for dm in range(DC):
            c.dma("q", outpart[g * D + dm * 128: g * D + (dm + 1) * 128, col0:col0 + S],
                  ostage[:, dm * S:(dm + 1) * S])
        c.v.wait(c.dma_sems["q"], c.semctr[c.dma_sems["q"]])
        c.t.wait(c.dma_sems["q"], c.semctr[c.dma_sems["q"]])
        c.p.wait(c.sem("mm"), c.semctr[c.sem("mm")])
        c.q.wait(c.sem("mm"), c.semctr[c.sem("mm")])
    c.wait_all_dma()
    c.barrier()
    c.fb("b3_xa", "b3_dlb", "b3_ygb", "b3_ost", "b3_wout", "b3_dtw")
    c.ff("b3_dA", "b3_dBu", "b3_h", "b3_Bb", "b3_Cb", "b3_y", "b3_w", "b3_delta",
         "b3_hcols", "b3_zs", "b3_negA", "b3_alogc", "b3_dtbc",
         "b3_dparc", "b3_cwt", "b3_cbt")
    c.fb("b3_bcrow", "b3_selm")
    if upto in (27, 28):
        pt = c.af("b3_pt27", TOK)
        for j in range(DC):
            t = c.dma("p", pt[:], outpart[j * 128:(j + 1) * 128, :])
            c.q.wait_tok(t)
            c.dma("q", x_out[j * 128:(j + 1) * 128, :], pt[:])
            c.barrier()
        return
    ctok = c.collective("ReduceScatter", ALU.add, GROUPS_BATCH, outpart, outred)
    c.p.wait_tok(ctok)
    c.barrier()
    if upto == 31:
        pt = c.af("b3_pt", TOK)
        for j in range(DC):
            t = c.dma("p", pt[:], outred[j * 128:(j + 1) * 128, :])
            c.q.wait_tok(t)
            c.dma("q", x_out[j * 128:(j + 1) * 128, :], pt[:])
            c.barrier()
        return

    xn2 = c.ab("b3_xn2", DC * TOK)
    for j in range(DC):
        c.dma("p", xn2[:, j * TOK:(j + 1) * TOK], xnd[j * 128:(j + 1) * 128, :])
    acc = c.af("b3_acc", 16384)
    gw, gw_sl = load_weight_bf16(c, c.dram("b3_gWT", [D, 8], F32, kind="ExternalInput"), "b3_gw", DC * 128, 8)
    ebt = load_f32(c, c.dram("b3_ebias", [128, 8], F32, kind="ExternalInput"), "b3_ebt", 128, 8)
    rtf = c.af("b3_rt", 128 * 6 + 64 + 16 * 2 + 128)
    lg = rtf[:, 0:128]
    msk2 = rtf[:, 128:256]
    scr2 = rtf[:, 256:384]
    dbuf2 = rtf[:, 384:512]
    sval = rtf[:, 512:640]
    probs = rtf[:, 640:768]
    wbuf2 = rtf[:, 768:832]
    mseg2 = rtf[:, 832:848]
    ssum = rtf[:, 848:864]
    neg30b = rtf[:, 864:992]
    c.v.memset(neg30b, -1e30)
    c.v.memset(acc[:, 0:16384], 0.0)
    c.barrier()
    for t in range(16):
        b = t % 4
        if c.bank_tok[b] is not None:
            c.t.wait_tok(c.bank_tok[b])
        h = None
        for k in range(DC):
            h = c.t.matmul(c.psum[b][:, 0:8],
                           xn2[:, k * TOK + t * 128: k * TOK + t * 128 + 128],
                           gw_sl(k, 0, 8), start=(k == 0), stop=(k == DC - 1))
        mtk = h.inc(ms)
        c.v.wait_tok(mtk)
        h2 = c.v.tensor_copy(lg[:, t * 8:(t + 1) * 8], c.psum[b][:, 0:8])
        c.bank_tok[b] = h2.inc(ev)
    c.barrier()
    x3 = lambda tns: tns.rearrange("p (cc e) -> p cc e", e=8)
    m3 = mseg2.rearrange("p (cc e) -> p cc e", e=1)
    seg_max(c, x3(lg), wbuf2.rearrange("p (cc e) -> p cc e", e=4), mseg2, 16, 8)
    a1, a2v = bass.broadcast_tensor_aps(x3(lg), m3)
    h2 = c.v.tensor_tensor(x3(probs), a1, a2v, ALU.subtract)
    vt = h2.inc(ev)
    c.s.wait_tok(vt)
    h2 = c.s.activation(probs, probs, AF.Exp)
    st = h2.inc(es)
    c.v.wait_tok(st)
    e3 = x3(probs)
    w3 = wbuf2.rearrange("p (cc e) -> p cc e", e=4)
    c.v.tensor_tensor(w3[:, :, 0:4], e3[:, :, 0:4], e3[:, :, 4:8], ALU.add)
    c.v.tensor_tensor(w3[:, :, 0:2], w3[:, :, 0:2], w3[:, :, 2:4], ALU.add)
    h2 = c.v.tensor_tensor(ssum.rearrange("p (cc e) -> p cc e", e=1),
                           w3[:, :, 0:1], w3[:, :, 1:2], ALU.add)
    vt = h2.inc(ev)
    c.s.wait_tok(vt)
    h2 = c.s.activation(ssum, ssum, AF.Copy)
    st = h2.inc(es)
    c.v.wait_tok(st)
    h2 = c.v.reciprocal(ssum, ssum)
    vt = h2.inc(ev)
    c.s.wait_tok(vt)
    h2 = c.s.activation(ssum, ssum, AF.Copy)
    st = h2.inc(es)
    c.v.wait_tok(st)
    s3 = ssum.rearrange("p (cc e) -> p cc e", e=1)
    a1, a2v = bass.broadcast_tensor_aps(x3(probs), s3)
    h2 = c.v.tensor_tensor(x3(probs), a1, a2v, ALU.mult)
    vt = h2.inc(ev)
    eb3 = ebt[:, 0:8].rearrange("p (cc e) -> p cc e", e=8)
    a1p, a2e = bass.broadcast_tensor_aps(x3(probs), eb3)
    h2 = c.v.tensor_tensor(x3(sval), a1p, a2e, ALU.add)
    h2.inc(ev)
    batched_topk(c, sval, msk2, scr2, dbuf2, wbuf2, mseg2, 16, 8, 2, neg30b)
    h2 = c.v.tensor_tensor(scr2, probs, msk2, ALU.mult)
    vt = h2.inc(ev)
    c.barrier()
    cwfm = c.af("b3_cwfm", TOK)
    for t in range(16):
        b = t % 4
        if c.bank_tok[b] is not None:
            c.t.wait_tok(c.bank_tok[b])
        h = c.t.transpose(c.psum[b][0:8, 0:128], scr2[:, t * 8:(t + 1) * 8], c.ident_f[:])
        mtk = h.inc(ms)
        c.v.wait_tok(mtk)
        h2 = c.v.tensor_copy(cwfm[0:8, t * 128:(t + 1) * 128], c.psum[b][0:8, 0:128])
        c.bank_tok[b] = h2.inc(ev)
    cwdram = c.dram("b3_cwdram", [8, TOK], F32)
    c.barrier()
    c.dma("q", cwdram, cwfm[0:8, :])
    c.wait_all_dma()
    c.barrier()
    c.ff("b3_rt", "b3_cwfm", "b3_ebt")
    H3 = 2816
    NH3 = H3 // 128
    w1d = c.dram("b3_ew1", [8 * NH3 * DC * 128, 128], F32, kind="ExternalInput")
    w3d = c.dram("b3_ew3", [8 * NH3 * DC * 128, 128], F32, kind="ExternalInput")
    w2d = c.dram("b3_ew2", [8 * H3, D], F32, kind="ExternalInput")
    he = c.ab("b3_he", NH3 * 512)
    xe = c.ab("b3_xe", DC * 512)
    w1s = c.ab("b3_w1s", 2 * 1024)
    w3s = c.ab("b3_w3s", 2 * 1024)
    w2s = c.ab("b3_w2s", 2 * 1024)
    cwbf = c.af("b3_cwb", TOK + TOK + 512 + 512)
    cwrow = cwbf[0:1, 0:TOK]
    cwb = cwbf[:, TOK:2 * TOK]
    s1b = cwbf[:, 2 * TOK:2 * TOK + 512]
    tmp5 = cwbf[:, 2 * TOK + 512:2 * TOK + 1024]
    for e in range(8):
        c.dma("q", cwrow, cwdram[e:e + 1, :])
        c.barrier()
        bcast_row_to(c, lambda n: cwrow[0:1, n * 512:(n + 1) * 512],
                     lambda n: cwb[:, n * 512:(n + 1) * 512], NT, banks=(6, 7),
                     evac_eng="v")
        c.barrier()
        for n in range(NT):
            h2 = None
            for k in range(DC):
                h2 = c.v.tensor_tensor(xe[:, k * 512:(k + 1) * 512],
                                       xn2[:, k * TOK + n * 512: k * TOK + n * 512 + 512],
                                       cwb[:, n * 512:(n + 1) * 512], ALU.mult)
            xet = h2.inc(ev)
            c.t.wait_tok(xet)
            wtoks = [None, None]
            for hc in range(NH3):
                sb = hc % 2
                if wtoks[sb] is not None:
                    c.p.wait_tok(wtoks[sb])
                base1 = (e * NH3 + hc) * DC * 128
                for kk2 in range(DC):
                    c.dma("p", w1s[:, sb * 1024 + kk2 * 128: sb * 1024 + (kk2 + 1) * 128],
                          w1d[base1 + kk2 * 128: base1 + (kk2 + 1) * 128, :])
                    dt = c.dma("p", w3s[:, sb * 1024 + kk2 * 128: sb * 1024 + (kk2 + 1) * 128],
                               w3d[base1 + kk2 * 128: base1 + (kk2 + 1) * 128, :])
                c.t.wait_tok(dt)
                b = 4 + sb
                b2 = 6 + sb
                if c.bank_tok[b] is not None:
                    c.t.wait_tok(c.bank_tok[b])
                if c.bank_tok[b2] is not None:
                    c.t.wait_tok(c.bank_tok[b2])
                h = None
                for k in range(DC):
                    h = c.t.matmul(c.psum[b][:, 0:512],
                                   w1s[:, sb * 1024 + k * 128: sb * 1024 + k * 128 + 128],
                                   xe[:, k * 512:(k + 1) * 512],
                                   start=(k == 0), stop=(k == DC - 1))
                mtk1 = h.inc(ms)
                for k in range(DC):
                    h = c.t.matmul(c.psum[b2][:, 0:512],
                                   w3s[:, sb * 1024 + k * 128: sb * 1024 + k * 128 + 128],
                                   xe[:, k * 512:(k + 1) * 512],
                                   start=(k == 0), stop=(k == DC - 1))
                mtk2 = h.inc(ms)
                wtoks[sb] = mtk2
                c.s.wait_tok(mtk1)
                h2 = c.s.activation(s1b[:, 0:512], c.psum[b][:, 0:512], AF.Silu)
                stok = h2.inc(es)
                c.bank_tok[b] = stok
                c.v.wait_tok(stok)
                c.v.wait_tok(mtk2)
                h2 = c.v.tensor_tensor(he[:, hc * 512:(hc + 1) * 512], s1b[:, 0:512],
                                       c.psum[b2][:, 0:512], ALU.mult)
                c.bank_tok[b2] = h2.inc(ev)
            c.barrier()
            wtoks = [None, None]
            for hc in range(NH3):
                sb = hc % 2
                if wtoks[sb] is not None:
                    c.p.wait_tok(wtoks[sb])
                dt = c.dma("p", w2s[:, sb * 1024:(sb + 1) * 1024],
                           w2d[e * H3 + hc * 128: e * H3 + hc * 128 + 128, :])
                c.t.wait_tok(dt)
                h = None
                for m in range(DC):
                    h = c.t.matmul(c.psum[m][:, 0:512],
                                   w2s[:, sb * 1024 + m * 128: sb * 1024 + m * 128 + 128],
                                   he[:, hc * 512:(hc + 1) * 512],
                                   start=(hc == 0), stop=(hc == NH3 - 1))
                wtoks[sb] = h.inc(ms)
            lastm = wtoks[(NH3 - 1) % 2]
            for m in range(DC):
                c.v.wait_tok(lastm)
                h2 = c.v.tensor_tensor(acc[:, m * TOK + n * 512: m * TOK + n * 512 + 512],
                                       acc[:, m * TOK + n * 512: m * TOK + n * 512 + 512],
                                       c.psum[m][:, 0:512], ALU.add)
                c.bank_tok[m] = h2.inc(ev)
            c.barrier()
    c.fb("b3_he", "b3_xe", "b3_w1s", "b3_w3s", "b3_w2s", "b3_gw", "b3_xn2")
    c.ff("b3_cwb")
    xj = c.af("b3_xj", TOK)
    af2 = c.af("b3_af", TOK)
    for j in range(DC):
        t1tok = c.dma("q", xj[:], x_in[j * 128:(j + 1) * 128, :])
        t2tok = c.dma("p", af2[:], outred[j * 128:(j + 1) * 128, :])
        c.v.wait_tok(t1tok)
        c.v.wait_tok(t2tok)
        c.v.tensor_tensor(xj[:], xj[:], acc[:, j * TOK:(j + 1) * TOK], ALU.add)
        h = c.v.tensor_tensor(xj[:], xj[:], af2[:], ALU.add)
        vt = h.inc(ev)
        c.q.wait_tok(vt)
        c.dma("q", x_out[j * 128:(j + 1) * 128, :], xj[:])
        c.barrier()
    c.ff("b3_xj", "b3_af", "b3_acc")
    c.barrier()


def prep_block3(inputs, core):
    A = np.ascontiguousarray
    p4 = core % 4
    c0 = p4 * 512
    d = {}
    wt = inputs["L3_attn_in_proj_w"].T
    d["b3_WinT"] = A(np.concatenate([wt[:, c0:c0 + 512], wt[:, 2048 + c0:2048 + c0 + 512]], axis=1))
    cw = inputs["L3_attn_conv_w"][c0:c0 + 512, 0, :]
    d["b3_convw"] = A(cw.reshape(4, 128, 4).transpose(1, 0, 2).reshape(128, 16))
    d["b3_convb"] = A(inputs["L3_attn_conv_b"][c0:c0 + 512].reshape(4, 128).T)
    d["b3_xprojT"] = A(inputs["L3_attn_x_proj_w"].T[c0:c0 + 512, :])
    d["b3_dtwT"] = A(inputs["L3_attn_dt_proj_w"].T[:, c0:c0 + 512])
    d["b3_dtb"] = A(inputs["L3_attn_dt_proj_b"][c0:c0 + 512].reshape(4, 128).T)
    d["b3_alog"] = A(inputs["L3_attn_A_log"][c0:c0 + 512].reshape(4, 128, 16).transpose(1, 0, 2).reshape(128, 64))
    d["b3_dpar"] = A(inputs["L3_attn_D"][c0:c0 + 512].reshape(4, 128).T)
    e32 = np.eye(32, dtype=np.float32)
    d["b3_sel"] = np_bf16(np.repeat(e32[:, :, None], 128, axis=2).reshape(32, 32 * 128))
    d["b3_WoutT"] = A(inputs["L3_attn_out_proj_w"].T[c0:c0 + 512, :])
    d["b3_gWT"] = A(inputs["L3_ffn_gate_w"].T)
    d["b3_ebias"] = A(np.broadcast_to(inputs["L3_ffn_expert_bias"], (128, 8)).copy())
    NH3 = 22
    w1 = inputs["L3_ffn_w1"].reshape(8, 8, 128, NH3, 128)
    d["b3_ew1"] = A(w1.transpose(0, 3, 1, 2, 4).reshape(8 * NH3 * 8 * 128, 128))
    w3 = inputs["L3_ffn_w3"].reshape(8, 8, 128, NH3, 128)
    d["b3_ew3"] = A(w3.transpose(0, 3, 1, 2, 4).reshape(8 * NH3 * 8 * 128, 128))
    d["b3_ew2"] = A(inputs["L3_ffn_w2"].reshape(8 * 2816, 1024))
    return d


def build_block3(upto=99):
    nc = bass.Bass("TRN2", target_bir_lowering=False)
    with contextlib.ExitStack() as stack:
        c = Ctx(nc, stack, ab_cols=45312, af_cols=24832)
        x_in = c.dram("x_in", [D, TOK], F32, kind="ExternalInput")
        x_out = c.dram("x_out", [D, TOK], F32, kind="ExternalOutput")
        init_consts(c)
        emit_block3(c, x_in, x_out, upto=upto)
        c.finalize()
    return nc


B1S = 128
B1NCH = SEQ // B1S


def emit_block1(c, x_in, x_out, upto=99):
    ev = c.sem("ev_v")
    es = c.sem("ev_s")
    ms = c.sem("mm")
    NST = 16

    xn = c.ab("b1_xn", DC * TOK)
    x_f = c.af("b1_x", 16384)
    for j in range(DC):
        c.dma("q", x_f[:, j * TOK:(j + 1) * TOK], x_in[j * 128:(j + 1) * 128, :])
    c.barrier()
    lntok = emit_layernorm(c, "b1ln", lambda j: x_f[:, j * TOK:(j + 1) * TOK],
                           lambda j: xn[:, j * TOK:(j + 1) * TOK])
    c.ff("b1_x")
    xnd = c.dram("b1_xnd", [D, TOK], BF16)
    c.q.wait_tok(lntok)
    for j in range(DC):
        c.dma("q", xnd[j * 128:(j + 1) * 128, :], xn[:, j * TOK:(j + 1) * TOK])
    c.wait_all_dma()
    c.barrier()
    gam = load_f32(c, c.dram("b1_lngc", [128, DC], F32, kind="ExternalInput"), "b1_gam", 128, DC)
    bet = load_f32(c, c.dram("b1_lnbc", [128, DC], F32, kind="ExternalInput"), "b1_bet", 128, DC)
    xn2a = c.ab("b1_xn2a", DC * TOK)
    c.barrier()
    ln2tok = emit_layernorm(c, "b1ln2", lambda j: xn[:, j * TOK:(j + 1) * TOK],
                            lambda j: xn2a[:, j * TOK:(j + 1) * TOK],
                            gamma_col=lambda j: gam[:, j:j + 1],
                            beta_col=lambda j: bet[:, j:j + 1],
                            x_is_bf16=True)
    xn2d = c.dram("b1_xn2d", [D, TOK], BF16)
    c.q.wait_tok(ln2tok)
    for j in range(DC):
        c.dma("q", xn2d[j * 128:(j + 1) * 128, :], xn2a[:, j * TOK:(j + 1) * TOK])
    c.wait_all_dma()
    c.barrier()
    c.fb("b1_xn", "b1_xn2a")
    c.ff("b1_gam", "b1_bet")
    xng = c.dram("b1_xng", [4 * D, TOK], BF16)
    ctok = c.collective("AllGather", ALU.bypass, GROUPS_BATCH, xn2d, xng)
    c.p.wait_tok(ctok)
    c.barrier()

    wind = c.dram("b1_WinT", [D, 1024], F32, kind="ExternalInput")
    xa = c.ab("b1_xa", 4 * B3CPAD)
    xs_buf = c.ab("b1_xs", 4096)
    zd = c.dram("b1_z", [512, SEQ], BF16)
    zstage = c.ab("b1_zst", 2048)
    for half in range(2):
        win = c.ab("b1_win", 4096)
        for k in range(DC):
            c.dma("p", win[:, k * 512:(k + 1) * 512],
                  wind[k * 128:(k + 1) * 128, half * 512:(half + 1) * 512])
        c.barrier()
        xstok = None
        for n in range(16):
            r, lc = n // 4, (n % 4) * 512
            if xstok is not None:
                c.q.wait_tok(xstok)
            dt = None
            for k in range(DC):
                dt = c.dma("q", xs_buf[:, k * 512:(k + 1) * 512],
                           xng[r * D + k * 128: r * D + k * 128 + 128, lc:lc + 512])
            c.t.wait_tok(dt)
            h = None
            zt = None
            for m in range(4):
                b = m % 4
                if c.bank_tok[b] is not None:
                    c.t.wait_tok(c.bank_tok[b])
                for k in range(DC):
                    h = c.t.matmul(c.psum[b][:, 0:512],
                                   win[:, k * 512 + m * 128: k * 512 + m * 128 + 128],
                                   xs_buf[:, k * 512:(k + 1) * 512],
                                   start=(k == 0), stop=(k == DC - 1))
                mtk = h.inc(ms)
                xstok = mtk
                c.v.wait_tok(mtk)
                if half == 0:
                    h2 = c.v.tensor_copy(
                        xa[:, m * B3CPAD + 3 + n * 512: m * B3CPAD + 3 + n * 512 + 512],
                        c.psum[b][:, 0:512])
                    c.bank_tok[b] = h2.inc(ev)
                else:
                    h2 = c.v.tensor_copy(zstage[:, m * 512:(m + 1) * 512], c.psum[b][:, 0:512])
                    zt = h2.inc(ev)
                    c.bank_tok[b] = zt
            if half == 1:
                c.q.wait_tok(zt)
                for mz in range(4):
                    c.dma("q", zd[mz * 128:(mz + 1) * 128, n * 512:(n + 1) * 512],
                          zstage[:, mz * 512:(mz + 1) * 512])
                c.v.wait(c.dma_sems["q"], c.semctr[c.dma_sems["q"]])
        c.barrier()
        c.fb("b1_win")
    c.fb("b1_xs", "b1_zst")
    for cc in range(4):
        c.v.memset(xa[:, cc * B3CPAD:cc * B3CPAD + 3], 0.0)
    cwt = load_f32(c, c.dram("b1_convw", [128, 16], F32, kind="ExternalInput"), "b1_cwt", 128, 16)
    cbt = load_f32(c, c.dram("b1_convb", [128, 4], F32, kind="ExternalInput"), "b1_cbt", 128, 4)
    cacc = c.af("b1_cacc", SEQ)
    c.barrier()
    stok_prev = None
    for cc in range(4):
        base = cc * B3CPAD
        if stok_prev is not None:
            c.v.wait_tok(stok_prev)
        c.v.tensor_scalar(cacc[:, 0:SEQ], xa[:, base + 0: base + SEQ],
                          cwt[:, cc * 4 + 0: cc * 4 + 1], None, ALU.mult)
        for k in range(1, 4):
            h2 = c.v.scalar_tensor_tensor(cacc[:, 0:SEQ], xa[:, base + k: base + k + SEQ],
                                          cwt[:, cc * 4 + k: cc * 4 + k + 1],
                                          cacc[:, 0:SEQ], ALU.mult, ALU.add)
        vt = h2.inc(ev)
        c.s.wait_tok(vt)
        h2 = c.s.activation(xa[:, base + 3: base + 3 + SEQ], cacc[:, 0:SEQ],
                            AF.Silu, bias=cbt[:, cc:cc + 1])
        stok_prev = h2.inc(es)
    c.barrier()
    c.ff("b1_cacc")
    xpw, xpw_sl = load_weight_bf16(c, c.dram("b1_xprojT", [512, 33], F32, kind="ExternalInput"), "b1_xpw", 512, 33)
    xp_part = c.dram("b1_xppart", [33, SEQ], F32)
    xp_red = c.dram("b1_xpred", [33, SEQ], F32)
    xstage = c.af("b1_xstage", 1024)
    c.barrier()
    for n in range(16):
        b = n % 2
        if c.bank_tok[b] is not None:
            c.t.wait_tok(c.bank_tok[b])
        h = None
        for cc in range(4):
            h = c.t.matmul(c.psum[b][0:33, 0:512], xpw_sl(cc, 0, 33),
                           xa[:, cc * B3CPAD + 3 + n * 512: cc * B3CPAD + 3 + n * 512 + 512],
                           start=(cc == 0), stop=(cc == 3))
        mtk = h.inc(ms)
        c.v.wait_tok(mtk)
        h2 = c.v.tensor_copy(xstage[0:33, (n % 2) * 512:(n % 2) * 512 + 512], c.psum[b][0:33, 0:512])
        vt = h2.inc(ev)
        c.bank_tok[b] = vt
        c.q.wait_tok(vt)
        c.dma("q", xp_part[:, n * 512:(n + 1) * 512], xstage[0:33, (n % 2) * 512:(n % 2) * 512 + 512])
        c.v.wait(c.dma_sems["q"], c.semctr[c.dma_sems["q"]])
    c.wait_all_dma()
    c.barrier()
    c.ff("b1_xstage")
    c.fb("b1_xpw")
    ctok = c.collective("AllReduce", ALU.add, GROUPS_BATCH, xp_part, xp_red)
    c.p.wait_tok(ctok)
    c.barrier()
    S = B1S
    dtw, _dtw_sl = load_weight_bf16(c, c.dram("b1_dtwT", [33, 512], F32, kind="ExternalInput"), "b1_dtw", 33, 512)
    dtb = load_f32(c, c.dram("b1_dtb", [128, 4], F32, kind="ExternalInput"), "b1_dtbc", 128, 4)
    alog = load_f32(c, c.dram("b1_alog", [128, 64], F32, kind="ExternalInput"), "b1_alogc", 128, 64)
    dpar = load_f32(c, c.dram("b1_dpar", [128, 4], F32, kind="ExternalInput"), "b1_dparc", 128, 4)
    selm = c.ab("b1_selm", 32 * 128)
    c.dma("q", selm[0:33, :], c.dram("b1_sel", [33, 32 * 128], BF16, kind="ExternalInput"))
    woutT, wout_sl = load_weight_bf16(c, c.dram("b1_WoutT", [512, 1024], F32, kind="ExternalInput"), "b1_wout", 512, 1024)
    negA = c.af("b1_negA", 64)
    nrA = c.af("b1_nrA", 128)
    c.barrier()
    h2 = c.s.activation(negA[:, 0:64], alog[:, 0:64], AF.Exp)
    st = h2.inc(es)
    c.v.wait_tok(st)
    h2 = c.v.tensor_scalar(negA[:, 0:64], negA[:, 0:64], -1.0, None, ALU.mult)
    natok = h2.inc(ev)
    c.v.scalar_tensor_tensor(nrA[:, 64:128], negA[:, 0:64], 1e-8, negA[:, 0:64], ALU.add, ALU.bypass)
    h2 = c.v.reciprocal(nrA[:, 64:128], nrA[:, 64:128])
    vt = h2.inc(ev)
    c.s.wait_tok(vt)
    h2 = c.s.activation(nrA[:, 64:128], nrA[:, 64:128], AF.Copy)
    st = h2.inc(es)
    c.v.wait_tok(st)
    h2 = c.v.scalar_tensor_tensor(nrA[:, 0:64], nrA[:, 64:128], -1.0, nrA[:, 64:128], ALU.mult, ALU.bypass)
    vt = h2.inc(ev)
    c.s.wait_tok(natok)
    c.s.wait_tok(vt)
    c.s.activation(negA[:, 0:64], negA[:, 0:64], AF.Copy)
    c.barrier()
    dA = c.af("b1_dA", 64 * S)
    dBx = c.af("b1_dBx", 64 * S)
    hst = c.af("b1_hst", 64 * (S + 1))
    Bball = c.af("b1_Bb", NST * S)
    Cball = c.af("b1_Cb", NST * S)
    ycc4 = c.af("b1_y", 4 * S)
    wcc = c.af("b1_w", S)
    delta = c.af("b1_delta", 4 * S)
    zsil = c.af("b1_zs", S)
    neg10 = c.af("b1_n10", 64)
    xp33 = c.ab("b1_dlb", B1S)
    ygb = c.ab("b1_ygb", 4 * S)
    ostage = c.ab("b1_ost", DC * S)
    outpart = c.dram("b1_outpart", [4 * D, TOK], BF16)
    outred = c.dram("b1_outred", [D, TOK], BF16)
    c.v.memset(hst[:, 0:64 * (S + 1)], 0.0)
    c.v.memset(neg10[:, 0:64], -10.0)
    c.barrier()
    dA3 = dA.rearrange("p (j s) -> p j s", s=S)
    dBx3 = dBx.rearrange("p (j s) -> p j s", s=S)
    h3 = hst.rearrange("p (j s) -> p j s", s=S + 1)
    n103 = neg10.rearrange("p (j s) -> p j s", s=1)
    ytk = None
    scantok = None
    for ch in range(B1NCH):
        t0 = ch * S
        dt1 = c.dma("p", xp33[0:33, 0:S], xp_red[0:33, t0:t0 + S])
        c.t.wait_tok(dt1)
        for n in range(2 * NST):
            b = 4 + n % 2
            if c.bank_tok[b] is not None:
                c.t.wait_tok(c.bank_tok[b])
            h = c.t.matmul(c.psum[b][:, 0:S], selm[0:33, n * 128:(n + 1) * 128],
                           xp33[0:33, 0:S], start=True, stop=True)
            mtk = h.inc(ms)
            dst = Bball if n < NST else Cball
            nn = n % NST
            c.s.wait_tok(mtk)
            h2 = c.s.activation(dst[:, nn * S:(nn + 1) * S], c.psum[b][:, 0:S], AF.Copy)
            c.bank_tok[b] = h2.inc(es)
        for cc in range(4):
            b = cc % 2
            if c.bank_tok[b] is not None:
                c.t.wait_tok(c.bank_tok[b])
            h = c.t.matmul(c.psum[b][:, 0:S], dtw[0:33, cc * 128:(cc + 1) * 128],
                           xp33[0:33, 0:S], start=True, stop=True)
            mtk = h.inc(ms)
            dsl = delta[:, cc * S:(cc + 1) * S]
            c.s.wait_tok(mtk)
            c.s.activation(dsl, c.psum[b][:, 0:S], AF.Abs, bias=dtb[:, cc:cc + 1])
            c.s.activation(dsl, dsl, AF.Exp, scale=c.cc(-1.0))
            h2 = c.s.activation(dsl, dsl, AF.Ln, bias=c.cc(1.0))
            st = h2.inc(es)
            c.v.wait_tok(st)
            c.v.scalar_tensor_tensor(wcc[:, 0:S], c.psum[b][:, 0:S], dtb[:, cc:cc + 1],
                                     dsl, ALU.add, ALU.bypass)
            h2 = c.v.scalar_tensor_tensor(wcc[:, 0:S], wcc[:, 0:S], 0.0, dsl, ALU.max, ALU.bypass)
            c.bank_tok[b] = h2.inc(ev)
            c.v.tensor_tensor(dsl, dsl, wcc[:, 0:S], ALU.add)
            h2 = c.v.scalar_tensor_tensor(dsl, dsl, 0.001, dsl, ALU.add, ALU.bypass)
            vt = h2.inc(ev)
        c.s.wait_tok(vt)
        hs = None
        for cc in range(4):
            for nn in range(NST):
                j = cc * 16 + nn
                hs = c.s.activation(dA[:, j * S:(j + 1) * S], delta[:, cc * S:(cc + 1) * S],
                                    AF.Exp, scale=negA[:, j:j + 1])
        st = hs.inc(es)
        c.v.wait_tok(st)
        for cc in range(4):
            xasl = xa[:, cc * B3CPAD + 3 + t0: cc * B3CPAD + 3 + t0 + S]
            for nn in range(NST):
                j = cc * 16 + nn
                c.v.tensor_tensor(wcc[:, 0:S], xasl, Bball[:, nn * S:(nn + 1) * S], ALU.mult)
                c.v.scalar_tensor_tensor(dBx[:, j * S:(j + 1) * S],
                                         dA[:, j * S:(j + 1) * S], nrA[:, j:j + 1],
                                         wcc[:, 0:S], ALU.mult, ALU.bypass)
                h2 = c.v.scalar_tensor_tensor(dBx[:, j * S:(j + 1) * S],
                                              dBx[:, j * S:(j + 1) * S], nrA[:, 64 + j:65 + j],
                                              wcc[:, 0:S], ALU.add, ALU.mult)
        vt = h2.inc(ev)
        if ch > 0:
            c.v.scalar_tensor_tensor(h3[:, :, 0:1], h3[:, :, S:S + 1], 0.0,
                                     h3[:, :, S:S + 1], ALU.add, ALU.bypass)
        for t in range(S):
            c.v.tensor_tensor(h3[:, :, t + 1:t + 2], dA3[:, :, t:t + 1], h3[:, :, t:t + 1], ALU.mult)
            c.v.tensor_tensor(h3[:, :, t + 1:t + 2], h3[:, :, t + 1:t + 2], dBx3[:, :, t:t + 1], ALU.add)
            h2 = c.v.scalar_tensor_tensor(h3[:, :, t + 1:t + 2], h3[:, :, t + 1:t + 2], 10.0,
                                          n103[:, :, 0:1], ALU.min, ALU.max)
        scantok = h2.inc(ev)
        for cc in range(4):
            xasl = xa[:, cc * B3CPAD + 3 + t0: cc * B3CPAD + 3 + t0 + S]
            ysl = ycc4[:, cc * S:(cc + 1) * S]
            first = True
            for nn in range(NST):
                j = cc * 16 + nn
                hsl = hst[:, j * (S + 1) + 1: j * (S + 1) + 1 + S]
                if first:
                    c.v.tensor_tensor(ysl, hsl, Cball[:, nn * S:(nn + 1) * S], ALU.mult)
                    first = False
                else:
                    c.v.tensor_tensor(wcc[:, 0:S], hsl, Cball[:, nn * S:(nn + 1) * S], ALU.mult)
                    c.v.tensor_tensor(ysl, ysl, wcc[:, 0:S], ALU.add)
            h2 = c.v.scalar_tensor_tensor(ysl, xasl, dpar[:, cc:cc + 1], ysl, ALU.mult, ALU.add)
            vt = h2.inc(ev)
            if ch + cc > 0:
                c.p.wait_tok(ytk)
            zt = c.dma("p", zsil[:, 0:S], zd[cc * 128:(cc + 1) * 128, t0:t0 + S])
            c.s.wait_tok(zt)
            c.s.wait_tok(vt)
            h2 = c.s.activation(zsil[:, 0:S], zsil[:, 0:S], AF.Silu)
            st = h2.inc(es)
            c.v.wait_tok(st)
            h2 = c.v.tensor_tensor(ygb[:, cc * S:(cc + 1) * S], ysl, zsil[:, 0:S], ALU.mult)
            ytk = h2.inc(ev)
        c.t.wait_tok(ytk)
        lasth = [None] * DC
        for dm in range(DC):
            if c.bank_tok[dm] is not None:
                c.t.wait_tok(c.bank_tok[dm])
            for cc2 in range(4):
                lasth[dm] = c.t.matmul(c.psum[dm][:, 0:S], wout_sl(cc2, dm * 128, 128),
                                       ygb[:, cc2 * S:(cc2 + 1) * S],
                                       start=(cc2 == 0), stop=(cc2 == 3))
        dmt = None
        for dm in range(DC):
            mtk = lasth[dm].inc(ms)
            c.v.wait_tok(mtk)
            h2 = c.v.tensor_copy(ostage[:, dm * S:(dm + 1) * S], c.psum[dm][:, 0:S])
            c.bank_tok[dm] = h2.inc(ev)
            dmt = c.bank_tok[dm]
        c.q.wait_tok(dmt)
        g = t0 // TOK
        col0 = t0 % TOK
        for dm in range(DC):
            c.dma("q", outpart[g * D + dm * 128: g * D + (dm + 1) * 128, col0:col0 + S],
                  ostage[:, dm * S:(dm + 1) * S])
        c.v.wait(c.dma_sems["q"], c.semctr[c.dma_sems["q"]])
        c.t.wait(c.dma_sems["q"], c.semctr[c.dma_sems["q"]])
        c.p.wait(c.sem("mm"), c.semctr[c.sem("mm")])
        c.q.wait(c.sem("mm"), c.semctr[c.sem("mm")])
    c.wait_all_dma()
    c.barrier()
    c.fb("b1_xa", "b1_dlb", "b1_ygb", "b1_ost", "b1_wout", "b1_dtw", "b1_selm")
    c.ff("b1_dA", "b1_dBx", "b1_hst", "b1_Bb", "b1_Cb", "b1_y", "b1_w", "b1_delta",
         "b1_zs", "b1_n10", "b1_negA", "b1_nrA", "b1_alogc", "b1_dtbc",
         "b1_dparc", "b1_cwt", "b1_cbt")
    ctok = c.collective("ReduceScatter", ALU.add, GROUPS_BATCH, outpart, outred)
    c.p.wait_tok(ctok)
    c.barrier()
    if upto == 31:
        pt = c.af("b1_pt", TOK)
        for j in range(DC):
            t = c.dma("p", pt[:], outred[j * 128:(j + 1) * 128, :])
            c.q.wait_tok(t)
            c.dma("q", x_out[j * 128:(j + 1) * 128, :], pt[:])
            c.barrier()
        return

    xn2 = c.ab("b1_xnf", DC * TOK)
    for j in range(DC):
        c.dma("p", xn2[:, j * TOK:(j + 1) * TOK], xnd[j * 128:(j + 1) * 128, :])
    acc = c.af("b1_acc", 16384)
    rw, rw_sl = load_weight_bf16(c, c.dram("b1_rW", [D, 64], F32, kind="ExternalInput"), "b1_rw", DC * 128, 64)
    rbt = load_f32(c, c.dram("b1_rbias", [128, 64], F32, kind="ExternalInput"), "b1_rbt", 128, 64)
    E1 = 64
    rtf = c.af("b1_rt", 1024 * 5 + 512 + 16 * 2 + 1024)
    lg = rtf[:, 0:1024]
    msk2 = rtf[:, 1024:2048]
    scr2 = rtf[:, 2048:3072]
    dbuf2 = rtf[:, 3072:4096]
    probs = rtf[:, 4096:5120]
    wbuf2 = rtf[:, 5120:5632]
    mseg2 = rtf[:, 5632:5648]
    ssum = rtf[:, 5648:5664]
    neg30b = rtf[:, 5664:6688]
    c.v.memset(neg30b, -1e30)
    c.barrier()
    for t in range(16):
        b = t % 4
        if c.bank_tok[b] is not None:
            c.t.wait_tok(c.bank_tok[b])
        h = None
        for k in range(DC):
            h = c.t.matmul(c.psum[b][:, 0:64],
                           xn2[:, k * TOK + t * 128: k * TOK + t * 128 + 128],
                           rw_sl(k, 0, 64), start=(k == 0), stop=(k == DC - 1))
        mtk = h.inc(ms)
        c.v.wait_tok(mtk)
        h2 = c.v.tensor_tensor(lg[:, t * 64:(t + 1) * 64], c.psum[b][:, 0:64],
                               rbt[:, 0:64], ALU.add)
        c.bank_tok[b] = h2.inc(ev)
    c.barrier()
    x3 = lambda tns: tns.rearrange("p (cc e) -> p cc e", e=E1)
    m3 = mseg2.rearrange("p (cc e) -> p cc e", e=1)
    seg_max(c, x3(lg), wbuf2.rearrange("p (cc e) -> p cc e", e=E1 // 2), mseg2, 16, E1)
    a1, a2v = bass.broadcast_tensor_aps(x3(lg), m3)
    h2 = c.v.tensor_tensor(x3(probs), a1, a2v, ALU.subtract)
    vt = h2.inc(ev)
    c.s.wait_tok(vt)
    h2 = c.s.activation(probs, probs, AF.Exp)
    st = h2.inc(es)
    c.v.wait_tok(st)
    e3 = x3(probs)
    w3 = wbuf2.rearrange("p (cc e) -> p cc e", e=E1 // 2)
    ww = E1 // 2
    h2 = c.v.tensor_tensor(w3[:, :, 0:ww], e3[:, :, 0:ww], e3[:, :, ww:2 * ww], ALU.add)
    while ww > 1:
        ww //= 2
        h2 = c.v.tensor_tensor(w3[:, :, 0:ww], w3[:, :, 0:ww], w3[:, :, ww:2 * ww], ALU.add)
    h2 = c.v.tensor_tensor(ssum.rearrange("p (cc e) -> p cc e", e=1),
                           w3[:, :, 0:1], w3[:, :, 0:1], ALU.max)
    vt = h2.inc(ev)
    c.s.wait_tok(vt)
    h2 = c.s.activation(ssum, ssum, AF.Copy)
    st = h2.inc(es)
    c.v.wait_tok(st)
    h2 = c.v.reciprocal(ssum, ssum)
    vt = h2.inc(ev)
    c.s.wait_tok(vt)
    h2 = c.s.activation(ssum, ssum, AF.Copy)
    st = h2.inc(es)
    c.v.wait_tok(st)
    s3 = ssum.rearrange("p (cc e) -> p cc e", e=1)
    a1, a2v = bass.broadcast_tensor_aps(x3(probs), s3)
    h2 = c.v.tensor_tensor(x3(probs), a1, a2v, ALU.mult)
    h2.inc(ev)
    batched_topk(c, probs, msk2, scr2, dbuf2, wbuf2, mseg2, 16, E1, 8, neg30b)
    h2 = c.v.tensor_tensor(scr2, probs, msk2, ALU.mult)
    vt = h2.inc(ev)
    sc3 = x3(scr2)
    ww = E1 // 2
    c.v.tensor_tensor(w3[:, :, 0:ww], sc3[:, :, 0:ww], sc3[:, :, ww:2 * ww], ALU.add)
    while ww > 1:
        ww //= 2
        c.v.tensor_tensor(w3[:, :, 0:ww], w3[:, :, 0:ww], w3[:, :, ww:2 * ww], ALU.add)
    h2 = c.v.tensor_tensor(ssum.rearrange("p (cc e) -> p cc e", e=1),
                           w3[:, :, 0:1], w3[:, :, 0:1], ALU.max)
    vt = h2.inc(ev)
    c.s.wait_tok(vt)
    h2 = c.s.activation(ssum, ssum, AF.Copy)
    st = h2.inc(es)
    c.v.wait_tok(st)
    h2 = c.v.reciprocal(ssum, ssum)
    vt = h2.inc(ev)
    c.s.wait_tok(vt)
    h2 = c.s.activation(ssum, ssum, AF.Copy)
    st = h2.inc(es)
    c.v.wait_tok(st)
    a1, a2v = bass.broadcast_tensor_aps(x3(scr2), s3)
    h2 = c.v.tensor_tensor(x3(scr2), a1, a2v, ALU.mult)
    vt = h2.inc(ev)
    c.barrier()
    cwfm = c.af("b1_cwfm", TOK)
    for t in range(16):
        b = t % 4
        if c.bank_tok[b] is not None:
            c.t.wait_tok(c.bank_tok[b])
        h = c.t.transpose(c.psum[b][0:64, 0:128], scr2[:, t * 64:(t + 1) * 64], c.ident_f[:])
        mtk = h.inc(ms)
        c.v.wait_tok(mtk)
        h2 = c.v.tensor_copy(cwfm[0:64, t * 128:(t + 1) * 128], c.psum[b][0:64, 0:128])
        c.bank_tok[b] = h2.inc(ev)
    cwdram = c.dram("b1_cwdram", [64, TOK], F32)
    cwfmb = c.ab("b1_cwfmb", TOK)
    h2 = c.v.tensor_copy(cwfmb[0:64, :], cwfm[0:64, :])
    h2.inc(ev)
    c.barrier()
    c.dma("q", cwdram, cwfm[0:64, :])
    b2w, b2w_sl = load_weight_bf16(c, c.dram("b1_eb2", [64, D], F32, kind="ExternalInput"), "b1_eb2s", 64, D)
    c.v.memset(acc[:, 0:16384], 0.0)
    c.barrier()

    def ev_add_acc1(m, n, ps, mtk):
        c.v.wait_tok(mtk)
        h2 = c.v.tensor_tensor(acc[:, m * TOK + n * 512: m * TOK + n * 512 + 512],
                               acc[:, m * TOK + n * 512: m * TOK + n * 512 + 512],
                               ps[:, 0:512], ALU.add)
        return h2.inc(ev)

    gemm(c, 1, DC, NT,
         lambda k, m: b2w[0:64, m * 128:(m + 1) * 128],
         lambda k, n: cwfmb[0:64, n * 512:(n + 1) * 512],
         ev_add_acc1)
    c.barrier()
    c.fb("b1_cwfmb", "b1_eb2s", "b1_rw")
    c.ff("b1_rt", "b1_cwfm", "b1_rbt")
    w1d = c.dram("b1_ew1", [64 * DC * 128, 256], F32, kind="ExternalInput")
    w2d = c.dram("b1_ew2", [64 * 2 * 128, D], F32, kind="ExternalInput")
    b1d = c.dram("b1_eb1c", [64 * 128, 2], F32, kind="ExternalInput")
    he = c.ab("b1_he", 2 * TOK)
    w1e = c.ab("b1_w1e", DC * 256)
    w2e = c.ab("b1_w2e", 2 * D)
    cwbf = c.af("b1_cwb", TOK + TOK + 2 + 512)
    cwrow = cwbf[0:1, 0:TOK]
    cwb = cwbf[:, TOK:2 * TOK]
    rb1e = cwbf[:, 2 * TOK:2 * TOK + 2]
    ut1 = cwbf[:, 2 * TOK + 2:2 * TOK + 2 + 512]
    for e in range(E1):
        for k in range(DC):
            c.dma("p", w1e[:, k * 256:(k + 1) * 256],
                  w1d[(e * DC + k) * 128:(e * DC + k + 1) * 128, :])
        for k in range(2):
            c.dma("p", w2e[:, k * D:(k + 1) * D],
                  w2d[(e * 2 + k) * 128:(e * 2 + k + 1) * 128, :])
        c.dma("q", rb1e, b1d[e * 128:(e + 1) * 128, :])
        c.dma("q", cwrow, cwdram[e:e + 1, :])
        c.barrier()
        bcast_row_to(c, lambda n: cwrow[0:1, n * 512:(n + 1) * 512],
                     lambda n: cwb[:, n * 512:(n + 1) * 512], NT, banks=(6, 7),
                     evac_eng="v")

        def ev_he1(m, n, ps, mtk):
            c.s.wait_tok(mtk)
            h2 = c.s.activation(ut1[:, 0:512], ps[:, 0:512], AF.Silu, bias=rb1e[:, m:m + 1])
            stok = h2.inc(es)
            c.v.wait_tok(stok)
            h2 = c.v.tensor_tensor(he[:, m * TOK + n * 512: m * TOK + n * 512 + 512],
                                   ut1[:, 0:512], cwb[:, n * 512:(n + 1) * 512], ALU.mult)
            h2.inc(ev)
            return stok

        gemm(c, DC, 2, NT,
             lambda k, m: w1e[:, k * 256 + m * 128: k * 256 + m * 128 + 128],
             lambda k, n: xn2[:, k * TOK + n * 512: k * TOK + n * 512 + 512], ev_he1)
        c.barrier()
        gemm(c, 2, DC, NT,
             lambda k, m: w2e[:, k * D + m * 128: k * D + m * 128 + 128],
             lambda k, n: he[:, k * TOK + n * 512: k * TOK + n * 512 + 512],
             ev_add_acc1)
        c.barrier()
    c.fb("b1_he", "b1_w1e", "b1_w2e")
    c.ff("b1_cwb")
    sw1d = c.dram("b1_sw1", [2 * 16 * DC * 128, 128], F32, kind="ExternalInput")
    sw2d = c.dram("b1_sw2", [2 * 2048, D], F32, kind="ExternalInput")
    sb1d = c.dram("b1_sb1c", [2 * 128, 16], F32, kind="ExternalInput")
    sb2d = c.dram("b1_sb2c", [2 * 128, DC], F32, kind="ExternalInput")
    hsh = c.ab("b1_hsh", 16 * 512)
    w1s = c.ab("b1_w1s", 2 * 1024)
    w2s = c.ab("b1_w2s", 2 * 1024)
    sbf = c.af("b1_sbf", 16 + DC)
    sb1e = sbf[:, 0:16]
    sb2e = sbf[:, 16:16 + DC]
    for e in range(2):
        c.dma("q", sb1e, sb1d[e * 128:(e + 1) * 128, :])
        c.dma("q", sb2e, sb2d[e * 128:(e + 1) * 128, :])
        c.barrier()
        for n in range(NT):
            wtoks = [None, None]
            for hc in range(16):
                sb = hc % 2
                if wtoks[sb] is not None:
                    c.p.wait_tok(wtoks[sb])
                base1 = (e * 16 + hc) * DC * 128
                for kk2 in range(DC):
                    dt = c.dma("p", w1s[:, sb * 1024 + kk2 * 128: sb * 1024 + (kk2 + 1) * 128],
                               sw1d[base1 + kk2 * 128: base1 + (kk2 + 1) * 128, :])
                c.t.wait_tok(dt)
                b = 4 + sb
                if c.bank_tok[b] is not None:
                    c.t.wait_tok(c.bank_tok[b])
                h = None
                for k in range(DC):
                    h = c.t.matmul(c.psum[b][:, 0:512],
                                   w1s[:, sb * 1024 + k * 128: sb * 1024 + k * 128 + 128],
                                   xn2[:, k * TOK + n * 512: k * TOK + n * 512 + 512],
                                   start=(k == 0), stop=(k == DC - 1))
                mtk = h.inc(ms)
                wtoks[sb] = mtk
                c.s.wait_tok(mtk)
                h2 = c.s.activation(hsh[:, hc * 512:(hc + 1) * 512], c.psum[b][:, 0:512],
                                    AF.Silu, bias=sb1e[:, hc:hc + 1])
                c.bank_tok[b] = h2.inc(es)
            c.barrier()
            wtoks = [None, None]
            for hc in range(16):
                sb = hc % 2
                if wtoks[sb] is not None:
                    c.p.wait_tok(wtoks[sb])
                dt = c.dma("p", w2s[:, sb * 1024:(sb + 1) * 1024],
                           sw2d[e * 2048 + hc * 128: e * 2048 + hc * 128 + 128, :])
                c.t.wait_tok(dt)
                h = None
                for m in range(DC):
                    h = c.t.matmul(c.psum[m][:, 0:512],
                                   w2s[:, sb * 1024 + m * 128: sb * 1024 + m * 128 + 128],
                                   hsh[:, hc * 512:(hc + 1) * 512],
                                   start=(hc == 0), stop=(hc == 15))
                wtoks[sb] = h.inc(ms)
            lastm = wtoks[15 % 2]
            for m in range(DC):
                c.v.wait_tok(lastm)
                h2 = c.v.scalar_tensor_tensor(
                    acc[:, m * TOK + n * 512: m * TOK + n * 512 + 512],
                    c.psum[m][:, 0:512], sb2e[:, m:m + 1],
                    acc[:, m * TOK + n * 512: m * TOK + n * 512 + 512], ALU.add, ALU.add)
                c.bank_tok[m] = h2.inc(ev)
            c.barrier()
    c.fb("b1_hsh", "b1_w1s", "b1_w2s", "b1_xnf")
    c.ff("b1_sbf")
    xj = c.af("b1_xj", TOK)
    af2 = c.af("b1_af", TOK)
    for j in range(DC):
        t1tok = c.dma("q", xj[:], x_in[j * 128:(j + 1) * 128, :])
        t2tok = c.dma("p", af2[:], outred[j * 128:(j + 1) * 128, :])
        c.v.wait_tok(t1tok)
        c.v.wait_tok(t2tok)
        c.v.tensor_tensor(xj[:], xj[:], acc[:, j * TOK:(j + 1) * TOK], ALU.add)
        h = c.v.tensor_tensor(xj[:], xj[:], af2[:], ALU.add)
        vt = h.inc(ev)
        c.q.wait_tok(vt)
        c.dma("q", x_out[j * 128:(j + 1) * 128, :], xj[:])
        c.barrier()
    c.ff("b1_xj", "b1_af", "b1_acc")
    c.barrier()


def prep_block1(inputs, core):
    A = np.ascontiguousarray
    p4 = core % 4
    c0 = p4 * 512
    d = {}
    d["b1_lngc"] = A(inputs["L1_attn_ln_gamma"].reshape(DC, 128).T)
    d["b1_lnbc"] = A(inputs["L1_attn_ln_beta"].reshape(DC, 128).T)
    wt = inputs["L1_attn_W_in"].T
    d["b1_WinT"] = A(np.concatenate([wt[:, c0:c0 + 512], wt[:, 2048 + c0:2048 + c0 + 512]], axis=1))
    cw = inputs["L1_attn_conv_w"][c0:c0 + 512, 0, :]
    d["b1_convw"] = A(cw.reshape(4, 128, 4).transpose(1, 0, 2).reshape(128, 16))
    d["b1_convb"] = A(inputs["L1_attn_conv_b"][c0:c0 + 512].reshape(4, 128).T)
    d["b1_xprojT"] = A(inputs["L1_attn_W_xproj"].T[c0:c0 + 512, :])
    d["b1_dtwT"] = A(inputs["L1_attn_W_dt"].T[:, c0:c0 + 512])
    d["b1_dtb"] = A(inputs["L1_attn_b_dt"][c0:c0 + 512].reshape(4, 128).T)
    d["b1_alog"] = A(inputs["L1_attn_A_log"][c0:c0 + 512].reshape(4, 128, 16).transpose(1, 0, 2).reshape(128, 64))
    d["b1_dpar"] = A(inputs["L1_attn_D_param"][c0:c0 + 512].reshape(4, 128).T)
    sel = np.zeros((33, 32, 128), np.float32)
    for n in range(16):
        sel[1 + n, n, :] = 1.0
        sel[17 + n, 16 + n, :] = 1.0
    d["b1_sel"] = np_bf16(sel.reshape(33, 32 * 128))
    d["b1_WoutT"] = A(inputs["L1_attn_W_out"].T[c0:c0 + 512, :])
    d["b1_rW"] = A(inputs["L1_ffn_router_w"])
    d["b1_rbias"] = A(np.broadcast_to(inputs["L1_ffn_router_b"], (128, 64)).copy())
    d["b1_ew1"] = A(inputs["L1_ffn_w1"].reshape(64 * DC * 128, 256))
    d["b1_ew2"] = A(inputs["L1_ffn_w2"].reshape(64 * 2 * 128, D))
    d["b1_eb1c"] = A(inputs["L1_ffn_b1"].reshape(64, 2, 128).transpose(0, 2, 1).reshape(64 * 128, 2))
    d["b1_eb2"] = A(inputs["L1_ffn_b2"])
    sw1 = inputs["L1_ffn_sw1"].reshape(2, 8, 128, 16, 128)
    d["b1_sw1"] = A(sw1.transpose(0, 3, 1, 2, 4).reshape(2 * 16 * 8 * 128, 128))
    d["b1_sw2"] = A(inputs["L1_ffn_sw2"].reshape(2 * 2048, D))
    d["b1_sb1c"] = A(inputs["L1_ffn_sb1"].reshape(2, 16, 128).transpose(0, 2, 1).reshape(2 * 128, 16))
    d["b1_sb2c"] = A(inputs["L1_ffn_sb2"].reshape(2, DC, 128).transpose(0, 2, 1).reshape(2 * 128, DC))
    return d


def build_block1(upto=99):
    nc = bass.Bass("TRN2", target_bir_lowering=False)
    with contextlib.ExitStack() as stack:
        c = Ctx(nc, stack, ab_cols=43520, af_cols=30720)
        x_in = c.dram("x_in", [D, TOK], F32, kind="ExternalInput")
        x_out = c.dram("x_out", [D, TOK], F32, kind="ExternalOutput")
        init_consts(c)
        emit_block1(c, x_in, x_out, upto=upto)
        c.finalize()
    return nc
